# version 19
# baseline (speedup 1.0000x reference)
import math
import contextlib
import numpy as np
import ml_dtypes
import concourse.bass as bass
import concourse.mybir as mybir
from concourse.bass_utils import run_bass_kernel_spmd

F32 = mybir.dt.float32
BF16 = mybir.dt.bfloat16
I32 = mybir.dt.int32
AF = mybir.ActivationFunctionType
ALU = mybir.AluOpType
AX = mybir.AxisListType
BF = ml_dtypes.bfloat16

NCORES = 8
TOK = 2048
NT = TOK // 128
D = 1024
DC = D // 128
NH = 16
HD = 64
LBATCH = 4096
NBT = LBATCH // 128
LN_EPS = 1e-5
COS_EPS = 1e-4
L1_LN_EPS = 1e-12



class Sem:
    def __init__(self, nc, stack, name):
        self.h = stack.enter_context(nc.semaphore(name))
        self.n = 0

    def now(self):
        return (self, self.n)


class Prog:

    ENGINES = ("sync", "gpsimd", "tensor", "vector", "scalar")

    def __init__(self, nc, stack):
        self.nc = nc
        self.stack = stack
        self.stream = {e: [] for e in self.ENGINES}
        self.esem = {}

    def sem(self, name):
        return Sem(self.nc, self.stack, name)

    def emit(self, engine, fn, wait=(), inc=None, inc_k=1):
        wait = list(wait)
        es = self.esem.get(engine)
        if es is not None:
            wait.append(es.now())
            if inc is None:
                inc = es
        if inc is not None:
            inc.n += inc_k
        self.stream[engine].append((fn, tuple(wait), inc, inc_k))

    def dma(self, queue, out, in_, sem, wait=()):
        wait = list(wait)
        if queue == "gpsimd":
            wait.append(sem.now())
        self.emit(queue, lambda g, o=out, i=in_: g.dma_start(out=o, in_=i),
                  wait=wait, inc=sem, inc_k=16)

    def finalize(self):
        with self.nc.Block() as block:
            for eng_name in self.ENGINES:
                entries = self.stream[eng_name]
                if not entries:
                    continue

                def make(entries):
                    def f(eng):
                        for fn, wait, inc, k in entries:
                            for s, v in wait:
                                if v > 0:
                                    eng.wait_ge(s.h, v)
                            inst = fn(eng)
                            if inc is not None:
                                inst.then_inc(inc.h, k)
                    return f

                getattr(block, eng_name)(make(entries))


def _ln_stats(p, ld_wait, src_ap, sq_scr, stats, g0, v_sem, a_sem, eps, inv_n,
              extra_r_scale=None):
    c = lambda i, w=1: stats[:, g0 + i:g0 + i + w]
    p.emit("vector", lambda v: v.reduce_sum(c(0), src_ap, axis=AX.X),
           wait=ld_wait, inc=v_sem)
    p.emit("scalar", lambda s: s.activation(sq_scr, src_ap, AF.Square,
                                            accum_out=c(1)),
           wait=ld_wait, inc=a_sem)
    p.emit("vector", lambda v: v.tensor_scalar(c(2, 2), c(0, 2), 1.0 / 1024,
                                               None, ALU.mult),
           wait=[a_sem.now()], inc=v_sem)
    p.emit("vector", lambda v: v.tensor_tensor(c(4, 2), c(2, 2), c(2, 2),
                                               ALU.mult), inc=v_sem)
    p.emit("vector", lambda v: v.tensor_tensor(c(6, 2), c(3, 2), c(4, 2),
                                               ALU.subtract), inc=v_sem)
    p.emit("vector", lambda v: v.tensor_scalar(c(0, 2), c(6, 2), 0.0,
                                               float(eps), ALU.max,
                                               ALU.add), inc=v_sem)
    p.emit("scalar", lambda s: s.activation(c(4, 2), c(0, 2), AF.Ln),
           wait=[v_sem.now()], inc=a_sem)
    p.emit("scalar", lambda s: s.activation(c(6, 2), c(4, 2), AF.Exp,
                                            scale=-0.5), inc=a_sem)
    if extra_r_scale is not None:
        p.emit("vector", lambda v: v.tensor_scalar(
            c(6, 2), c(6, 2), float(extra_r_scale), None, ALU.mult),
            wait=[a_sem.now()], inc=v_sem)
    return c(2), c(6)



class SemBundle:
    pass


def make_sems(p):
    S = SemBundle()
    S.s_ld = [p.sem("s_ld0"), p.sem("s_ld1")]
    S.s_w = p.sem("s_w")
    S.s_g = p.sem("s_g")
    S.s_mm = p.sem("s_mm")
    S.s_v = p.sem("s_v")
    S.s_a = p.sem("s_a")
    S.s_st = [p.sem("s_st0"), p.sem("s_st1")]
    S.s_x = p.sem("s_x")
    S.s_kv = p.sem("s_kv")
    S.s_ctx = p.sem("s_ctx")
    p.S = S
    p.esem = {"vector": S.s_v, "scalar": S.s_a}
    return S


def barrier(p):
    S = p.S
    snap = [s.now() for s in
            [S.s_ld[0], S.s_ld[1], S.s_w, S.s_g, S.s_mm, S.s_v, S.s_a,
             S.s_st[0], S.s_st[1], S.s_x, S.s_kv, S.s_ctx] if s.n > 0]
    for eng in Prog.ENGINES:
        p.emit(eng, lambda e: e.nop(), wait=snap)



def _sincos_tables():
    l = np.arange(LBATCH, dtype=np.float64)
    ang = (np.pi / 2) * (l + 1.0) / LBATCH
    sin = np.sin(ang).astype(np.float32).reshape(NBT, 128).T.copy()
    cos = np.cos(ang).astype(np.float32).reshape(NBT, 128).T.copy()
    return sin, cos


IDENT_BF = np.eye(128, dtype=BF)

KV_SLOT = []
for _h in range(NH):
    _b = 0 if _h < 5 else (1 if _h < 10 else 2)
    KV_SLOT.append((_b, 512 * _b + 65 * (_h - (0, 5, 10)[_b])))
KV_BANK_FIRST = {0: 0, 1: 5, 2: 10}
KV_BANK_LAST = {0: 4, 1: 9, 2: 15}


def tr_rotate(p, ident, bufs, pairs, extra_wait=(), first_wait=None,
              alt_evac=True):
    S = p.S
    if not hasattr(p, "_tr_last"):
        p._tr_last = {}
    evn = {}
    eng_of = {}
    nb = len(bufs)
    for c, (sap, dap) in enumerate(pairs):
        buf = bufs[c % nb]
        if c == 0:
            w = (list(first_wait) if first_wait is not None
                 else [(S.s_v, S.s_v.n)])
            w += list(extra_wait)
        else:
            w = []
        if c >= nb:
            sem = S.s_v if eng_of[c - nb] == "vector" else S.s_a
            w.append((sem, evn[c - nb]))
        elif id(buf) in p._tr_last:
            last = p._tr_last[id(buf)]
            w.extend(last if isinstance(last, list) else [last])
        p.emit("tensor", lambda te, sap=sap, buf=buf: te.transpose(
            buf[:, 0:128], sap, ident[:]), wait=w, inc=S.s_mm)
        eng = "vector" if (not alt_evac or c % 2 == 0) else "scalar"
        if eng == "vector":
            p.emit("vector", lambda v, dap=dap, buf=buf: v.tensor_copy(
                dap, buf[:, 0:128]), wait=[S.s_mm.now()])
            evn[c] = S.s_v.n
            p._tr_last[id(buf)] = (S.s_v, S.s_v.n)
        else:
            p.emit("scalar", lambda s, dap=dap, buf=buf: s.activation(
                dap, buf[:, 0:128], AF.Copy), wait=[S.s_mm.now()])
            evn[c] = S.s_a.n
            p._tr_last[id(buf)] = (S.s_a, S.s_a.n)
        eng_of[c] = eng


def tr_done(p):
    S = p.S
    return [(S.s_v, S.s_v.n), (S.s_a, S.s_a.n)]


def tr_batch(p, ident, bank, srcs, contig_dst=None, first_wait=None,
             extra_wait=(), evacs=None, evac_engines=("vector", "scalar")):
    S = p.S
    if not hasattr(p, "_tr_last"):
        p._tr_last = {}
    n = len(srcs)
    w0 = (list(first_wait) if first_wait is not None
          else [(S.s_v, S.s_v.n), (S.s_a, S.s_a.n)])
    w0 += list(extra_wait)
    if id(bank) in p._tr_last:
        last = p._tr_last[id(bank)]
        w0 += last if isinstance(last, list) else [last]
    for i, sap in enumerate(srcs):
        p.emit("tensor", lambda te, sap=sap, i=i: te.matmul(
            bank[:, 128 * i:128 * (i + 1)], sap, ident[:],
            start=(i == 0), stop=(i == n - 1), is_transpose=True),
            wait=(w0 if i == 0 else []),
            inc=(S.s_mm if i == n - 1 else None))
    mmdone = S.s_mm.now()
    if evacs is None:
        ne = len(evac_engines)
        per = (n * 128) // ne
        evacs = [(contig_dst[:, per * j:per * (j + 1)],
                  bank[:, per * j:per * (j + 1)]) for j in range(ne)]
    marks = []
    for j, (dap, sap) in enumerate(evacs):
        eng = evac_engines[j % len(evac_engines)]
        if eng == "vector":
            p.emit("vector", lambda v, dap=dap, sap=sap: v.tensor_copy(
                dap, sap), wait=[mmdone])
            marks.append((S.s_v, S.s_v.n))
        else:
            p.emit("scalar", lambda s, dap=dap, sap=sap: s.activation(
                dap, sap, AF.Copy), wait=[mmdone])
            marks.append((S.s_a, S.s_a.n))
    p._tr_last[id(bank)] = marks
    return marks


def tr_views(bank, dst_fm, n, t, stride):
    dv = dst_fm.rearrange("p (c f) -> p c f", f=stride)
    sv = bank[:, 0:n * 128].rearrange("p (c f) -> p c f", f=128)
    h = n // 2
    return [(dv[:, 0:h, 128 * t:128 * (t + 1)], sv[:, 0:h, :]),
            (dv[:, h:n, 128 * t:128 * (t + 1)], sv[:, h:n, :])]



def declare_io_A(nc, scratch=None):
    io = {}
    io["xb"] = nc.dram_tensor("Axb", [LBATCH, D], F32, kind="ExternalInput").ap()
    io["wqt"] = nc.dram_tensor("Awqt", [D, D], F32, kind="ExternalInput").ap()
    io["wkt"] = nc.dram_tensor("Awkt", [D, D], F32, kind="ExternalInput").ap()
    io["wvt"] = nc.dram_tensor("Awvt", [D, D], F32, kind="ExternalInput").ap()
    io["wot"] = nc.dram_tensor("Awot", [D, D], F32, kind="ExternalInput").ap()
    io["bqpp"] = nc.dram_tensor("Abqpp", [128, DC], F32, kind="ExternalInput").ap()
    io["bkr"] = nc.dram_tensor("Abkr", [1, D], F32, kind="ExternalInput").ap()
    io["bvr"] = nc.dram_tensor("Abvr", [1, D], F32, kind="ExternalInput").ap()
    io["bor"] = nc.dram_tensor("Abor", [1, D], F32, kind="ExternalInput").ap()
    io["sinp"] = nc.dram_tensor("Asinp", [128, NBT], F32, kind="ExternalInput").ap()
    io["cosp"] = nc.dram_tensor("Acosp", [128, NBT], F32, kind="ExternalInput").ap()
    io["x1o"] = (scratch["x1o"] if scratch and "x1o" in scratch else nc.dram_tensor("Ax1", [TOK, D], F32, kind="ExternalOutput").ap())
    return io


def emit_A(nc, st, p, io):
    xb = io["xb"]
    wqt = io["wqt"]
    wkt = io["wkt"]
    wvt = io["wvt"]
    wot = io["wot"]
    bqpp = io["bqpp"]
    bkr = io["bkr"]
    bvr = io["bvr"]
    bor = io["bor"]
    sinp = io["sinp"]
    cosp = io["cosp"]
    x1o = io["x1o"]
    sb = lambda name, shape, dt=BF16: st.enter_context(
        nc.sbuf_tensor("AA" + name, shape, dt))
    lnxT = sb("lnxT", [128, DC * LBATCH])
    qfm = sb("qfm", [128, DC * TOK])
    wA = sb("wA", [128, DC * D])
    wB = sb("wB", [128, DC * D])
    kvs = sb("kvs", [128, NH * 65])
    kvw = sb("kvw", [128, NH * 65])
    ident = sb("ident", [128, 128])
    ones1 = sb("ones1", [1, 128])
    bq_s = sb("bq_s", [128, DC], F32)
    bk_s = sb("bk_s", [1, D])
    bv_s = sb("bv_s", [1, D])
    bo_s = sb("bo_s", [1, D])
    sin_s = sb("sin_s", [128, NBT], F32)
    cos_s = sb("cos_s", [128, NBT], F32)
    xt = [sb(f"xt{i}", [128, D], F32) for i in range(2)]
    lnx = [sb(f"lnx{i}", [128, D]) for i in range(2)]
    ktl = [sb(f"ktl{i}", [128, NH * 128]) for i in range(2)]
    vtl = [sb(f"vtl{i}", [128, NH * 65]) for i in range(2)]
    acmb = [sb(f"acmb{i}", [128, NH * 65], F32) for i in range(2)]
    abf = [sb(f"abf{i}", [128, D]) for i in range(2)]
    atT = [sb(f"atT{i}", [128, D]) for i in range(2)]
    hsb = [sb(f"hsb{i}", [128, D], F32) for i in range(2)]
    dens = [sb("dens0", [128, NH], F32), sb("dens1", [128, NH], F32)]
    zrec = [sb("zrec0", [128, NH], F32), sb("zrec1", [128, NH], F32)]
    sq_scr = sb("sq_scr", [128, D])
    stats = sb("stats", [128, 16], F32)
    ps = lambda name, shape, dt=F32: st.enter_context(
        nc.psum_tensor("AA" + name, shape, dt))
    psum_kv = ps("psum_kv", [128, 1536])
    psum_q = ps("psum_q", [128, 1024])
    p_tr = ps("p_tr", [128, 1024], BF16)
    ptrB_guard = nc.psum_tensor("Ap_trB", [128, 1024], BF16)
    p_trB = ptrB_guard.__enter__()

    S = p.S
    s_ld = S.s_ld
    s_w = S.s_w
    s_g = S.s_g
    s_kv = S.s_kv
    s_mm = S.s_mm
    s_v = S.s_v
    s_a = S.s_a
    s_st = S.s_st
    s_x = S.s_x

    p.dma("sync", ident[:], nc.inline_tensor(IDENT_BF, "identA").ap(), s_w)
    p.dma("sync", bq_s[:], bqpp, s_w)
    p.dma("sync", sin_s[:], sinp, s_w)
    p.dma("sync", cos_s[:], cosp, s_w)
    ident_ready = s_w.now()
    p.dma("gpsimd", bk_s[:], bkr, s_g)
    p.dma("gpsimd", bv_s[:], bvr, s_g)
    p.dma("gpsimd", bo_s[:], bor, s_g)
    p.dma("gpsimd", wA[:].rearrange("p (c f) -> p c f", c=DC),
          wqt.rearrange("(c p) f -> p c f", p=128), s_g)
    p.dma("gpsimd", wB[:].rearrange("p (c f) -> p c f", c=DC),
          wkt.rearrange("(c p) f -> p c f", p=128), s_g)
    w_ready = s_g.now()
    p.emit("vector", lambda v: v.memset(ones1[:], 1.0), inc=s_v)
    for b in range(2):
        p.emit("vector", lambda v, b=b: v.memset(
            vtl[b][:].rearrange("p (h s) -> p h s", s=65)[:, :, 64:65], 1.0),
            inc=s_v)

    pend = None
    norm_done = {}
    for t in range(NBT):
        b = t % 2
        xwait = []
        if t >= 2:
            xwait = [(s_v, norm_done[t - 2]), (s_a, s_a.n)]
        p.dma("sync", xt[b][:], xb[128 * t:128 * (t + 1), :], s_ld[b],
              wait=xwait)
        ld = [s_ld[b].now()]
        if pend is not None:
            pend()
            pend = None
        m_ap, r_ap = _ln_stats(p, ld, xt[b][:], sq_scr[:], stats,
                               8 * (t % 2), s_v, s_a, LN_EPS, 1.0 / D)
        stats_a = s_a.now()

        def mk_norm(t=t, b=b, m_ap=m_ap, r_ap=r_ap, stats_a=stats_a):
            def go():
                p.emit("vector", lambda v: v.tensor_scalar(
                    lnx[b][:], xt[b][:], m_ap, r_ap,
                    ALU.subtract, ALU.mult), wait=[stats_a], inc=s_v)
                norm_done[t] = s_v.n
                bank = p_tr if t % 2 == 0 else p_trB
                tr_batch(p, ident, bank,
                         [lnx[b][:, 128 * c:128 * (c + 1)]
                          for c in range(DC)],
                         evacs=tr_views(bank, lnxT[:], DC, t, LBATCH),
                         evac_engines=("vector", "vector"),
                         extra_wait=[ident_ready],
                         first_wait=[(s_v, norm_done[t])])
            return go
        pend = mk_norm()
    pend()
    lnxT_done2 = tr_done(p)
    lnxT_done = lnxT_done2[0]

    for fc in range(DC):
        for th in range(2):
            for e in range(DC):
                for g in range(2):
                    p.emit("tensor", lambda te, fc=fc, th=th, e=e, g=g:
                           te.matmul(
                               psum_q[:, 512 * g:512 * (g + 1)],
                               wA[:, e * D + 128 * fc:e * D + 128 * (fc + 1)],
                               lnxT[:, e * LBATCH + 1024 * th + 512 * g:
                                    e * LBATCH + 1024 * th + 512 * (g + 1)],
                               start=(e == 0), stop=(e == DC - 1)),
                           wait=(lnxT_done2 + [w_ready, (s_a, s_a.n)]
                                 if (e == 0 and g == 0) else []),
                           inc=(s_mm if (e == DC - 1 and g == 1) else None))
            p.emit("scalar", lambda s, fc=fc, th=th: s.activation(
                qfm[:, fc * TOK + 1024 * th:fc * TOK + 1024 * (th + 1)],
                psum_q[:, 0:1024], AF.Relu, bias=bq_s[:, fc:fc + 1]),
                wait=[s_mm.now()], inc=s_a)
    q_done = s_a.now()
    q_mm_done = s_mm.now()
    p.dma("gpsimd", wA[:].rearrange("p (c f) -> p c f", c=DC),
          wvt.rearrange("(c p) f -> p c f", p=128), s_g,
          wait=[q_mm_done])
    wv_ready = s_g.now()
    ptrB_guard.__exit__(None, None, None)
    psum_v = st.enter_context(nc.psum_tensor("Apsum_v", [128, 1024], F32))

    kv_first = [True, True, True]
    kv_tile_done = {}
    pend_kv = None
    for t in range(NBT):
        b = t % 2
        kw = [(s_v, s_v.n), (s_a, s_a.n)]
        if t == 0:
            kw += [q_done, wv_ready]
        for half in range(2):
            for e in range(DC):
                p.emit("tensor", lambda te, t=t, e=e, half=half: te.matmul(
                    psum_q[:, 512 * half:512 * (half + 1)],
                    lnxT[:, e * LBATCH + 128 * t:e * LBATCH + 128 * (t + 1)],
                    wB[:, e * D + 512 * half:e * D + 512 * (half + 1)],
                    start=(e == 0), stop=False),
                    wait=(kw if (half == 0 and e == 0) else []))
            p.emit("tensor", lambda te, half=half: te.matmul(
                psum_q[:, 512 * half:512 * (half + 1)], ones1[:],
                bk_s[:, 512 * half:512 * (half + 1)], start=False,
                stop=True))
            for e in range(DC):
                p.emit("tensor", lambda te, t=t, e=e, half=half: te.matmul(
                    psum_v[:, 512 * half:512 * (half + 1)],
                    lnxT[:, e * LBATCH + 128 * t:e * LBATCH + 128 * (t + 1)],
                    wA[:, e * D + 512 * half:e * D + 512 * (half + 1)],
                    start=(e == 0), stop=False))
            p.emit("tensor", lambda te, half=half: te.matmul(
                psum_v[:, 512 * half:512 * (half + 1)], ones1[:],
                bv_s[:, 512 * half:512 * (half + 1)], start=False,
                stop=True), inc=(s_mm if half == 1 else None))
        proj_done = s_mm.now()
        if pend_kv is not None:
            pend_kv()
            pend_kv = None
        ewait = [proj_done]
        if t >= 2:
            ewait.append((s_mm, kv_tile_done[t - 2]))
        kview = psum_q[:].rearrange("p (h d) -> p h d", d=64)
        p.emit("scalar", lambda s, b=b, t=t, kview=kview: s.activation(
            ktl[b][:].rearrange("p (h d) -> p h d", d=128)[:, :, 0:64],
            kview, AF.Relu, scale=sin_s[:, t:t + 1]), wait=ewait)
        p.emit("scalar", lambda s, b=b, t=t, kview=kview: s.activation(
            ktl[b][:].rearrange("p (h d) -> p h d", d=128)[:, :, 64:128],
            kview, AF.Relu, scale=cos_s[:, t:t + 1]))
        k_evac = s_a.now()
        p.emit("vector", lambda v, b=b: v.tensor_copy(
            vtl[b][:].rearrange("p (h s) -> p h s", s=65)[:, :, 0:64],
            psum_v[:].rearrange("p (h d) -> p h d", d=64)), wait=ewait,
            inc=s_v)
        evac_done = s_v.now()

        def mk_kv(t=t, b=b, evac_done=evac_done, k_evac=k_evac):
            def go():
                for h in range(NH):
                    bank, off = KV_SLOT[h]
                    start = kv_first[bank]
                    kv_first[bank] = False
                    stop = (t == NBT - 1) and (KV_BANK_LAST[bank] == h)
                    p.emit("tensor", lambda te, h=h, off=off, start=start,
                           stop=stop, b=b: te.matmul(
                               psum_kv[:, off:off + 65],
                               ktl[b][:, 128 * h:128 * (h + 1)],
                               vtl[b][:, 65 * h:65 * (h + 1)],
                               start=start, stop=stop),
                           wait=([evac_done, k_evac] if h == 0 else []),
                           inc=(s_mm if h == NH - 1 else None))
                kv_tile_done[t] = s_mm.n
            return go
        pend_kv = mk_kv()
    pend_kv()
    pend_kv = None
    kv_mm_done = s_mm.now()
    p.dma("gpsimd", wB[:].rearrange("p (c f) -> p c f", c=DC),
          wot.rearrange("(c p) f -> p c f", p=128), s_g,
          wait=[kv_mm_done])
    wo_ready = s_g.now()

    for bank, (o_dst, width) in enumerate(((0, 325), (325, 325),
                                           (650, 390))):
        p.emit("vector", lambda v, bank=bank, o_dst=o_dst, width=width:
               v.tensor_copy(kvs[:, o_dst:o_dst + width],
                             psum_kv[:, 512 * bank:512 * bank + width]),
               wait=([kv_mm_done] if bank == 0 else []), inc=s_v)
    kv_evac = s_v.now()
    p.dma("sync", kvw[0:64, :], kvs[64:128, :], s_kv, wait=[kv_evac])
    p.dma("sync", kvw[64:128, :], kvs[0:64, :], s_kv)
    kv_ready = s_kv.now()

    def attn_slot(h, which):
        j = h // 2
        if h % 2 == 0:
            if j < 7:
                off = (0 if which == "top" else 512) + 65 * j
                return (psum_kv, off, ("kv", off // 512))
            return (psum_kv, 1024 + (0 if which == "top" else 65),
                    ("kv", 2))
        if j < 7:
            off = (0 if which == "top" else 512) + 65 * j
            return (psum_v, off, ("v", off // 512))
        return (psum_q, 512 + (0 if which == "top" else 65), ("q", 1))

    store_done = {}
    comb_psum_done = {}
    comb_a_done = {}
    wo_evac_done = {}
    pendA = None
    for t in range(NT):
        b = t % 2
        if t == 0:
            aw = [kv_ready, q_done, (s_v, s_v.n), (s_mm, s_mm.n)]
        else:
            aw = [(s_v, comb_psum_done[t - 1]), (s_a, comb_a_done[t - 1])]
        mm_list = []
        for h in range(NH):
            for which in ("top", "bot"):
                mm_list.append((h, which) + attn_slot(h, which))
        first_bank = {}
        last_bank = {}
        for i, (h, which, bp, boff, bank_id) in enumerate(mm_list):
            first_bank.setdefault(bank_id, i)
            last_bank[bank_id] = i
        for i, (h, which, bp, boff, bank_id) in enumerate(mm_list):
            par = (h % 2) * 64
            qsl = qfm[par:par + 64,
                      (h // 2) * TOK + 128 * t:(h // 2) * TOK + 128 * (t + 1)]
            if which == "top":
                src = kvs if h % 2 == 0 else kvw
            else:
                src = kvw if h % 2 == 0 else kvs
            p.emit("tensor", lambda te, qsl=qsl, src=src, par=par, h=h,
                   bp=bp, boff=boff,
                   start=(first_bank[bank_id] == i),
                   stop=(last_bank[bank_id] == i):
                   te.matmul(bp[:, boff:boff + 65], qsl,
                             src[par:par + 64, 65 * h:65 * (h + 1)],
                             start=start, stop=stop,
                             tile_position=(par, 0)),
                   wait=(aw if i == 0 else []),
                   inc=(s_mm if i == len(mm_list) - 1 else None))
        amm = s_mm.now()

        def a_view(b, par, j0, n):
            ap = acmb[b][:].rearrange("p (j s) -> p j s", s=130)
            return ap[:, j0:j0 + n, 65 * par:65 * par + 65]

        def ps_view(pt, off, n):
            if n > 1:
                return pt[:, off:off + 65 * n].rearrange(
                    "p (j s) -> p j s", s=65)
            return pt[:, off:off + 65]

        act_done = {}
        for par, pt, spill_p, spill_off in ((0, psum_kv, psum_kv, 1024),
                                            (1, psum_v, psum_q, 512)):
            p.emit("scalar", lambda s, b=b, par=par, pt=pt, t=t:
                   s.activation(acmb[b][:, 520 * par:520 * par + 455],
                                pt[:, 0:455],
                                AF.Copy, scale=sin_s[:, t:t + 1]),
                   wait=([amm, (s_v, s_v.n)] if par == 0 else []))
            p.emit("scalar", lambda s, b=b, par=par, sp=spill_p,
                   so=spill_off, t=t: s.activation(
                       acmb[b][:, 520 * par + 455:520 * par + 520],
                       sp[:, so:so + 65],
                       AF.Copy, scale=sin_s[:, t:t + 1]))
            act_done[par] = s_a.n
        for par, pt, spill_p, spill_off in ((0, psum_kv, psum_kv, 1024),
                                            (1, psum_v, psum_q, 512)):
            p.emit("vector", lambda v, b=b, par=par, pt=pt, t=t:
                   v.scalar_tensor_tensor(
                       acmb[b][:, 520 * par:520 * par + 455],
                       pt[:, 512:512 + 455],
                       cos_s[:, t:t + 1],
                       acmb[b][:, 520 * par:520 * par + 455],
                       ALU.mult, ALU.add),
                   wait=[(s_a, act_done[par])] + ([amm] if par == 0 else []),
                   inc=s_v)
            p.emit("vector", lambda v, b=b, par=par, sp=spill_p,
                   so=spill_off, t=t: v.scalar_tensor_tensor(
                       acmb[b][:, 520 * par + 455:520 * par + 520],
                       sp[:, so + 65:so + 130],
                       cos_s[:, t:t + 1],
                       acmb[b][:, 520 * par + 455:520 * par + 520],
                       ALU.mult, ALU.add), inc=s_v)
        comb_psum_done[t] = s_v.n
        comb_a_done[t] = s_a.n
        p.emit("vector", lambda v, b=b: v.tensor_scalar(
            dens[b][:],
            acmb[b][:].rearrange("p (h s) -> p h s", s=65)[:, :, 64:65],
            COS_EPS, None, ALU.max), inc=s_v)
        p.emit("vector", lambda v, b=b: v.reciprocal(zrec[b][:], dens[b][:]),
               inc=s_v)
        zr = s_v.now()
        for h in range(NH):
            sl = (h % 2) * 8 + h // 2
            if h % 2 == 0:
                p.emit("vector", lambda v, b=b, h=h, sl=sl: v.tensor_scalar(
                    abf[b][:, 64 * h:64 * (h + 1)],
                    acmb[b][:, 65 * sl:65 * sl + 64],
                    zrec[b][:, sl:sl + 1], None, ALU.mult), inc=s_v)
            else:
                p.emit("scalar", lambda s, b=b, h=h, sl=sl: s.activation(
                    abf[b][:, 64 * h:64 * (h + 1)],
                    acmb[b][:, 65 * sl:65 * sl + 64],
                    AF.Copy, scale=zrec[b][:, sl:sl + 1]),
                    wait=([zr] if h == 1 else []))
        attn_done = s_v.now()
        attn_done_a = s_a.now()
        if pendA is not None:
            pendA()
            pendA = None

        def mk_tailA(t=t, b=b, attn_done=attn_done, attn_done_a=attn_done_a):
            def go():
                tr_batch(p, ident, p_tr,
                         [abf[b][:, 128 * c:128 * (c + 1)]
                          for c in range(DC)],
                         atT[b][:],
                         first_wait=[attn_done, attn_done_a])
                att_T2 = tr_done(p)
                for half in range(2):
                    if half == 0:
                        wo_w = att_T2 + [wo_ready]
                        if t >= 2:
                            wo_w.append((s_a, wo_evac_done[t - 2]))
                        elif t == 0:
                            wo_w.append((s_a, s_a.n))
                    else:
                        wo_w = [(s_a, s_a.n)]
                    for c in range(DC):
                        p.emit("tensor", lambda te, b=b, c=c, half=half:
                               te.matmul(psum_q[:, 0:512],
                                         atT[b][:, 128 * c:128 * (c + 1)],
                                         wB[:, c * D + 512 * half:
                                            c * D + 512 * (half + 1)],
                                         start=(c == 0), stop=False),
                               wait=(wo_w if c == 0 else []))
                    p.emit("tensor", lambda te, half=half: te.matmul(
                        psum_q[:, 0:512], ones1[:],
                        bo_s[:, 512 * half:512 * (half + 1)],
                        start=False, stop=True), inc=s_mm)
                    hw = [s_mm.now()]
                    if t >= 2 and half == 0:
                        hw.append((s_st[t % 2], store_done[t - 2]))
                    p.emit("scalar", lambda s, b=b, half=half: s.activation(
                        hsb[b][:, 512 * half:512 * (half + 1)],
                        psum_q[:, 0:512], AF.Copy),
                        wait=hw)
                wo_evac_done[t] = s_a.n
                h_done = s_a.now()
                p.dma("sync", xt[b][:], xb[128 * t:128 * (t + 1), :], s_x,
                      wait=[h_done])
                m_ap, r_ap = _ln_stats(p, [h_done], hsb[b][:], sq_scr[:],
                                       stats, 8 * (t % 2), s_v, s_a, LN_EPS,
                                       1.0 / D)
                p.emit("vector", lambda v, b=b, m_ap=m_ap, r_ap=r_ap:
                       v.tensor_scalar(hsb[b][:], hsb[b][:], m_ap, r_ap,
                                       ALU.subtract, ALU.mult),
                       wait=[s_a.now()], inc=s_v)
                p.emit("vector", lambda v, b=b: v.tensor_tensor(
                    hsb[b][:], hsb[b][:], xt[b][:], ALU.add),
                    wait=[(s_x, s_x.n)], inc=s_v)
                p.dma("sync", x1o[128 * t:128 * (t + 1), :], hsb[b][:],
                      s_st[b], wait=[s_v.now()])
                store_done[t] = s_st[b].n
            return go
        pendA = mk_tailA()
    pendA()
    p.emit("sync", lambda sy: sy.nop(),
           wait=[s_st[0].now(), s_st[1].now()])



def build_A():
    nc = bass.Bass("TRN2", target_bir_lowering=False, debug=False)
    io = declare_io_A(nc)
    st = contextlib.ExitStack()
    with st:
        p = Prog(nc, st)
        make_sems(p)
        emit_A(nc, st, p, io)
        p.finalize()
    return nc


def _prep_A(inputs, core):
    b = core // 2
    half = core % 2
    x = np.asarray(inputs["L0_attn_x"]).reshape(4, LBATCH, D)
    xb = x[b]
    sin, cos = _sincos_tables()
    if half == 1:
        xb = np.concatenate([xb[TOK:], xb[:TOK]], axis=0)
        sin = np.concatenate([sin[:, NT:], sin[:, :NT]], axis=1)
        cos = np.concatenate([cos[:, NT:], cos[:, :NT]], axis=1)
    return {
        "xb": np.ascontiguousarray(xb),
        "wqt": np.ascontiguousarray(np.asarray(inputs["L0_attn_Wq"]).T),
        "wkt": np.ascontiguousarray(np.asarray(inputs["L0_attn_Wk"]).T),
        "wvt": np.ascontiguousarray(np.asarray(inputs["L0_attn_Wv"]).T),
        "wot": np.ascontiguousarray(np.asarray(inputs["L0_attn_Wo"]).T),
        "bqpp": np.ascontiguousarray(
            np.asarray(inputs["L0_attn_bq"]).reshape(DC, 128).T),
        "bkr": np.asarray(inputs["L0_attn_bk"]).reshape(1, D),
        "bvr": np.asarray(inputs["L0_attn_bv"]).reshape(1, D),
        "bor": np.asarray(inputs["L0_attn_bo"]).reshape(1, D),
        "sinp": np.ascontiguousarray(sin),
        "cosp": np.ascontiguousarray(cos),
    }


def run_A(inputs):
    nc = build_A()
    in_maps = [{"A" + k: v for k, v in _prep_A(inputs, c).items()} for c in range(NCORES)]
    res = run_bass_kernel_spmd(nc, in_maps, core_ids=list(range(NCORES)))
    return np.concatenate([r["x1"] for r in res.results], axis=0)





SEL_BF = np.zeros((4, 8, 128), dtype=BF)
for _c in range(4):
    SEL_BF[_c, 2 * _c, 0:64] = 1
    SEL_BF[_c, 2 * _c + 1, 64:128] = 1

SEL64_BF = np.zeros((8, 8, 64), dtype=BF)
for _e in range(8):
    SEL64_BF[_e, _e, :] = 1


def declare_io_B(nc, scratch=None):
    io = {}
    io["x1i"] = (scratch["x1i"] if scratch and "x1i" in scratch else nc.dram_tensor("Bx1i", [TOK, D], F32, kind="ExternalInput").ap())
    io["w1at"] = nc.dram_tensor("Bw1at", [D, 512], F32, kind="ExternalInput").ap()
    io["w1b"] = nc.dram_tensor("Bw1b", [512, 4096], F32, kind="ExternalInput").ap()
    io["w2at"] = nc.dram_tensor("Bw2at", [4096, 512], F32, kind="ExternalInput").ap()
    io["w2b"] = nc.dram_tensor("Bw2b", [512, D], F32, kind="ExternalInput").ap()
    io["eww"] = nc.dram_tensor("Beww", [TOK, 2], F32, kind="ExternalInput").ap()
    io["eii"] = nc.dram_tensor("Beii", [TOK, 2], I32, kind="ExternalInput").ap()
    io["x2o"] = (scratch["x2o"] if scratch and "x2o" in scratch else nc.dram_tensor("Bx2", [TOK, D], F32, kind="ExternalOutput").ap())
    return io


def emit_B(nc, st, p, io):
    x1i = io["x1i"]
    w1at = io["w1at"]
    w1b = io["w1b"]
    w2at = io["w2at"]
    w2b = io["w2b"]
    eww = io["eww"]
    eii = io["eii"]
    x2o = io["x2o"]
    sb = lambda name, shape, dt=BF16: st.enter_context(
        nc.sbuf_tensor("BB" + name, shape, dt))
    w1at_s = sb("w1at_s", [128, DC * 512])
    w1b_s = sb("w1b_s", [128, 4 * 4096])
    w2at_s = sb("w2at_s", [128, 32 * 512])
    w2b_s = sb("w2b_s", [64, 8 * D])
    t1_sb = sb("t1_sb", [128, 4 * TOK])
    t3_sb = sb("t3_sb", [64, 8 * TOK])
    ident = sb("ident", [128, 128])
    sel64 = sb("sel64", [8, 8 * 64])
    eif = sb("eif", [128, NT * 2])
    ewf = sb("ewf", [128, NT * 2])
    c_all = sb("c_all", [128, NT * 8], F32)
    c_allb = sb("c_allb", [128, NT * 8])
    c_fm = sb("c_fm", [8, TOK])
    ceq = sb("ceq", [128, NT * 2])
    cpr = sb("cpr", [128, NT * 2])
    xt = [sb(f"xt{i}", [128, D], F32) for i in range(2)]
    lnx = [sb(f"lnx{i}", [128, D]) for i in range(2)]
    ysb = [sb(f"ysb{i}", [128, D], F32) for i in range(2)]
    sq_scr = sb("sq_scr", [128, D])
    stats = sb("stats", [128, 16], F32)
    lnxT_guard = nc.sbuf_tensor("BlnxT", [128, DC * TOK], BF16)
    lnxT = lnxT_guard.__enter__()
    ps = lambda name, shape, dt=F32: st.enter_context(
        nc.psum_tensor("BB" + name, shape, dt))
    psum2e = ps("psum2e", [128, 1024])
    psum2o = ps("psum2o", [128, 1024])
    ptr_guard = nc.psum_tensor("Bp_tr", [128, 1024], BF16)
    p_tr = ptr_guard.__enter__()
    ptrB_guard = nc.psum_tensor("Bp_trB", [128, 1024], BF16)
    p_trB = ptrB_guard.__enter__()

    S = p.S
    s_ld = S.s_ld
    s_w = S.s_w
    s_g = S.s_g
    s_mm = S.s_mm
    s_v = S.s_v
    s_a = S.s_a
    s_st = S.s_st
    s_x = S.s_x

    p.dma("sync", ident[:], nc.inline_tensor(IDENT_BF, "identB").ap(), s_w)
    p.dma("sync", sel64[:].rearrange("p (e m) -> p e m", e=8),
          nc.inline_tensor(
              np.ascontiguousarray(SEL64_BF.transpose(1, 0, 2)),
              "sel64B").ap(), s_w)
    consts_ready = s_w.now()
    p.dma("gpsimd", w1at_s[:].rearrange("p (c r) -> p c r", c=DC),
          w1at.rearrange("(c p) r -> p c r", p=128), s_g)
    p.dma("gpsimd", w1b_s[:].rearrange("p (c f) -> p c f", c=4),
          w1b.rearrange("(c p) f -> p c f", p=128), s_g)
    p.dma("gpsimd", w2b_s[:].rearrange("p (e f) -> p e f", e=8),
          w2b.rearrange("(e p) f -> p e f", p=64), s_g)
    for blk in range(4):
        p.dma("gpsimd",
              w2at_s[:, blk * 8 * 512:(blk + 1) * 8 * 512].rearrange(
                  "p (c r) -> p c r", c=8),
              w2at[blk * 1024:(blk + 1) * 1024, :].rearrange(
                  "(c p) r -> p c r", p=128), s_g)
    p.dma("gpsimd", eif[:].rearrange("p (t two) -> p t two", two=2),
          eii.rearrange("(t p) two -> p t two", p=128), s_g)
    p.dma("gpsimd", ewf[:].rearrange("p (t two) -> p t two", two=2),
          eww.rearrange("(t p) two -> p t two", p=128), s_g)
    w_ready = s_g.now()

    cw = [w_ready]
    for e in range(8):
        p.emit("vector", lambda v, e=e: v.tensor_scalar(
            ceq[:], eif[:], float(e), None, ALU.is_equal), wait=cw)
        cw = []
        p.emit("vector", lambda v: v.tensor_tensor(
            cpr[:], ceq[:], ewf[:], ALU.mult))
        p.emit("vector", lambda v, e=e: v.reduce_sum(
            c_all[:].rearrange("p (t e) -> p t e", e=8)[:, :, e:e + 1],
            cpr[:].rearrange("p (t d) -> p t d", d=2), axis=AX.X))
    p.emit("vector", lambda v: v.tensor_copy(c_allb[:], c_all[:]))

    for t in range(NT):
        b = t % 2
        xwait = [(s_v, s_v.n), (s_a, s_a.n)] if t >= 2 else []
        p.dma("sync", xt[b][:], x1i[128 * t:128 * (t + 1), :], s_ld[b],
              wait=xwait)
        ld = [s_ld[b].now()]
        m_ap, r_ap = _ln_stats(p, ld, xt[b][:], sq_scr[:], stats,
                               8 * (t % 2), s_v, s_a, LN_EPS, 1.0 / D)
        p.emit("vector", lambda v, b=b, m_ap=m_ap, r_ap=r_ap:
               v.tensor_scalar(lnx[b][:], xt[b][:], m_ap, r_ap,
                               ALU.subtract, ALU.mult),
               wait=[s_a.now()])
        tr_batch(p, ident, p_trB,
                 [lnx[b][:, 128 * c:128 * (c + 1)] for c in range(DC)],
                 evacs=tr_views(p_trB, lnxT[:], DC, t, TOK),
                 evac_engines=("vector", "vector"),
                 extra_wait=[consts_ready])
        p.emit("tensor", lambda te, t=t: te.transpose(
            p_tr[0:8, 512 + 128 * (t % 4):512 + 128 * (t % 4) + 128],
            c_allb[:, 8 * t:8 * (t + 1)], ident[:]),
            wait=[(s_v, s_v.n)], inc=s_mm)
        p.emit("vector", lambda v, t=t: v.tensor_copy(
            c_fm[0:8, 128 * t:128 * (t + 1)],
            p_tr[0:8, 512 + 128 * (t % 4):512 + 128 * (t % 4) + 128]),
            wait=[s_mm.now()])
    lnxT_done = s_v.now()
    ptr_mm_done = s_mm.now()

    crep_done = s_v.now()

    for rc in range(4):
        for tg in range(4):
            dst = psum2e if tg < 2 else psum2o
            col = 512 * (tg % 2)
            for e in range(DC):
                p.emit("tensor", lambda te, rc=rc, tg=tg, e=e, dst=dst,
                       col=col: te.matmul(
                           dst[:, col:col + 512],
                           w1at_s[:, e * 512 + 128 * rc:
                                  e * 512 + 128 * (rc + 1)],
                           lnxT[:, e * TOK + 512 * tg:
                                e * TOK + 512 * (tg + 1)],
                           start=(e == 0), stop=(e == DC - 1)),
                       wait=([crep_done, w_ready, (s_v, s_v.n),
                              (p.S.s_a, p.S.s_a.n)]
                             if (tg == 0 and e == 0) else []),
                       inc=(s_mm if e == DC - 1 else None))
        ev1 = [s_mm.now()]
        p.emit("vector", lambda v, rc=rc: v.tensor_copy(
            t1_sb[:, rc * TOK:rc * TOK + 1024], psum2e[:, 0:1024]),
            wait=ev1)
        p.emit("vector", lambda v, rc=rc: v.tensor_copy(
            t1_sb[:, rc * TOK + 1024:rc * TOK + 2048], psum2o[:, 0:1024]))
    t1_done = s_v.now()

    for e in range(8):
        for tg in range(4):
            p.emit("tensor", lambda te, e=e, tg=tg: te.matmul(
                psum2e[0:64, 512 * (tg % 2):512 * (tg % 2) + 512],
                sel64[:, 64 * e:64 * (e + 1)],
                c_fm[0:8, 512 * tg:512 * (tg + 1)],
                start=True, stop=True),
                wait=(([t1_done, consts_ready] if tg == 0 else []) +
                      ([(s_v, s_v.n)] if tg % 2 == 0 else [])),
                inc=(s_mm if tg % 2 == 1 else None))
            if tg % 2 == 1:
                p.emit("vector", lambda v, e=e, tg=tg: v.tensor_copy(
                    crep[0:64, e * TOK + 512 * (tg - 1):
                         e * TOK + 512 * (tg + 1)],
                    psum2e[0:64, 0:1024]), wait=[s_mm.now()])
    crep_ready = s_v.now()

    lnxT_guard.__exit__(None, None, None)
    hbuf = [st.enter_context(nc.sbuf_tensor(f"Bhbuf{i}", [128, 2048], BF16))
            for i in range(2)]
    crep = st.enter_context(nc.sbuf_tensor("Bcrep", [64, 8 * TOK], BF16))
    ptrB_guard.__exit__(None, None, None)
    ptr_guard.__exit__(None, None, None)
    p3e = st.enter_context(nc.psum_tensor("p3e", [64, 1024], F32))
    p3o = st.enter_context(nc.psum_tensor("p3o", [64, 1024], F32))

    gelu_done = {}
    gelu_done_par = {}
    t2mm_par = {}
    t3_done = {}
    pending_t3 = None
    it = 0
    for th in range(2):
        for c in range(4):
            p3_first = {("e", 0): True, ("e", 1): True,
                        ("o", 0): True, ("o", 1): True}
            pending_t3 = None
            for fc in range(32):
                for par, dst in ((0, psum2e), (64, psum2o)):
                    t2w = [t1_done, ptr_mm_done, crep_ready] if it == 0 \
                        else []
                    if it > 0:
                        t2w = [(s_a, gelu_done_par[(it - 1, par)])]
                    for g in range(2):
                        last = (par == 64 and g == 1)
                        p.emit("tensor", lambda te, c=c, fc=fc, th=th,
                               par=par, dst=dst, g=g: te.matmul(
                                   dst[:, 512 * g:512 * (g + 1)],
                                   w1b_s[par:par + 64,
                                         c * 4096 + 128 * fc:
                                         c * 4096 + 128 * (fc + 1)],
                                   t1_sb[par:par + 64,
                                         c * TOK + 1024 * th + 512 * g:
                                         c * TOK + 1024 * th + 512 * (g + 1)],
                                   start=True, stop=True,
                                   tile_position=(par, 0)),
                               wait=(t2w if g == 0 else []),
                               inc=(s_mm if (g == 1) else None))
                    t2mm_par[(it, par)] = s_mm.n
                t2mm = s_mm.now()
                if pending_t3 is not None:
                    pending_t3()
                    pending_t3 = None
                hb = hbuf[it % 2]
                gw = [(s_mm, t2mm_par[(it, 0)])]
                if it >= 2 and (it - 2) in t3_done:
                    gw.append((s_mm, t3_done[it - 2]))
                p.emit("scalar", lambda s, hb=hb: s.activation(
                    hb[:, 0:1024], psum2e[:, 0:1024], AF.Gelu), wait=gw)
                gelu_done_par[(it, 0)] = s_a.n
                p.emit("scalar", lambda s, hb=hb: s.activation(
                    hb[:, 1024:2048], psum2o[:, 0:1024], AF.Gelu),
                    wait=[(s_mm, t2mm_par[(it, 64)])])
                gelu_done_par[(it, 64)] = s_a.n
                gelu_done[it] = s_a.n

                def make_t3(c=c, fc=fc, it=it, hb=hb, pf=p3_first):
                    def go():
                        t3w = [(s_a, gelu_done[it])]
                        if fc == 0:
                            t3w.append((s_v, s_v.n))
                        for pk, dst3 in (("e", p3e), ("o", p3o)):
                            for g in range(2):
                                start = pf[(pk, g)]
                                pf[(pk, g)] = False
                                stop = (fc == 31)
                                last = (fc == 31 and pk == "o" and g == 1)
                                oddoff = 64 if pk == "o" else 0
                                p.emit("tensor", lambda te, c=c, fc=fc, g=g,
                                       dst3=dst3, oddoff=oddoff, hb=hb,
                                       start=start, stop=stop: te.matmul(
                                           dst3[0:64, 512 * g:512 * (g + 1)],
                                           w2at_s[:, fc * 512 + 128 * c +
                                                  oddoff:
                                                  fc * 512 + 128 * c +
                                                  oddoff + 64],
                                           hb[:, 1024 * (oddoff // 64) +
                                              512 * g:
                                              1024 * (oddoff // 64) +
                                              512 * (g + 1)],
                                           start=start, stop=stop),
                                       wait=(t3w if (pk == "e" and g == 0)
                                             else []),
                                       inc=(s_mm if last else None))
                        t3_done[it] = s_mm.n
                    return go
                pending_t3 = make_t3()
                it += 1
            pending_t3()
            pending_t3 = None
            p.emit("vector", lambda v, c=c, th=th: v.tensor_copy(
                t3_sb[0:64, (2 * c) * TOK + 1024 * th:
                      (2 * c) * TOK + 1024 * (th + 1)],
                p3e[0:64, 0:1024]), wait=[s_mm.now()])
            p.emit("vector", lambda v, c=c, th=th: v.tensor_copy(
                t3_sb[0:64, (2 * c + 1) * TOK + 1024 * th:
                      (2 * c + 1) * TOK + 1024 * (th + 1)],
                p3o[0:64, 0:1024]))
    t3_all = s_v.now()

    for e in range(8):
        p.emit("vector", lambda v, e=e: v.tensor_tensor(
            t3_sb[0:64, e * TOK:(e + 1) * TOK],
            t3_sb[0:64, e * TOK:(e + 1) * TOK],
            crep[0:64, e * TOK:(e + 1) * TOK], ALU.mult),
            wait=([t3_all, crep_ready] if e == 0 else []))
    t3s_done = s_v.now()

    store_done = {}
    for t in range(NT):
        b = t % 2
        for half in range(2):
            dst = psum2e if half == 0 else psum2o
            for e in range(8):
                w4 = [t3s_done, (s_v, s_v.n)] if e == 0 else []
                p.emit("tensor", lambda te, t=t, e=e, half=half, dst=dst:
                       te.matmul(
                           dst[:, 0:512],
                           t3_sb[0:64, e * TOK + 128 * t:
                                 e * TOK + 128 * (t + 1)],
                           w2b_s[0:64, e * D + 512 * half:
                                 e * D + 512 * (half + 1)],
                           start=(e == 0), stop=(e == 7)),
                       wait=w4, inc=(s_mm if e == 7 else None))
            hw0 = [s_mm.now()]
            if t >= 2:
                hw0.append((s_st[t % 2], store_done[t - 2]))
            p.emit("vector", lambda v, b=b, half=half, dst=dst:
                   v.tensor_copy(ysb[b][:, 512 * half:512 * (half + 1)],
                                 dst[:, 0:512]), wait=hw0)
        y_done = s_v.now()
        p.dma("sync", xt[b][:], x1i[128 * t:128 * (t + 1), :], s_x,
              wait=[y_done])
        m_ap, r_ap = _ln_stats(p, [y_done], ysb[b][:], sq_scr[:], stats,
                               8 * (t % 2), s_v, s_a, LN_EPS, 1.0 / D)
        p.emit("vector", lambda v, b=b, m_ap=m_ap, r_ap=r_ap:
               v.tensor_scalar(ysb[b][:], ysb[b][:], m_ap, r_ap,
                               ALU.subtract, ALU.mult),
               wait=[s_a.now()])
        p.emit("vector", lambda v, b=b: v.tensor_tensor(
            ysb[b][:], ysb[b][:], xt[b][:], ALU.add),
            wait=[(s_x, s_x.n)])
        p.dma("sync", x2o[128 * t:128 * (t + 1), :], ysb[b][:], s_st[b],
              wait=[s_v.now()])
        store_done[t] = s_st[b].n
    p.emit("sync", lambda sy: sy.nop(),
           wait=[s_st[0].now(), s_st[1].now()])


def build_B():
    nc = bass.Bass("TRN2", target_bir_lowering=False, debug=False)
    io = declare_io_B(nc)
    st = contextlib.ExitStack()
    with st:
        p = Prog(nc, st)
        make_sems(p)
        emit_B(nc, st, p, io)
        p.finalize()
    return nc


def _prep_B(inputs, core, x1=None):
    t0 = core * TOK
    ew = np.asarray(inputs["L0_ffn_expert_weights"])
    ei = np.asarray(inputs["L0_ffn_expert_indices"])
    n_active = ew.shape[0]
    if t0 + TOK <= n_active:
        ew_c, ei_c = ew[t0:t0 + TOK], ei[t0:t0 + TOK]
    else:
        ew_c = np.zeros((TOK, 2), np.float32)
        ei_c = np.zeros((TOK, 2), np.int32)
        if t0 < n_active:
            ew_c[:n_active - t0] = ew[t0:n_active]
            ei_c[:n_active - t0] = ei[t0:n_active]
    d = {
        "w1at": np.ascontiguousarray(np.asarray(inputs["L0_ffn_w1_A"]).T),
        "w1b": np.ascontiguousarray(np.asarray(inputs["L0_ffn_w1_B"])),
        "w2at": np.ascontiguousarray(np.asarray(inputs["L0_ffn_w2_A"]).T),
        "w2b": np.ascontiguousarray(np.asarray(inputs["L0_ffn_w2_B"])),
        "eww": np.ascontiguousarray(ew_c),
        "eii": np.ascontiguousarray(ei_c),
    }
    if x1 is not None:
        d["x1i"] = np.ascontiguousarray(x1[t0:t0 + TOK])
    return d


def run_B(inputs, x1):
    nc = build_B()
    in_maps = [{"B" + k: v for k, v in _prep_B(inputs, c, x1).items()} for c in range(NCORES)]
    res = run_bass_kernel_spmd(nc, in_maps, core_ids=list(range(NCORES)))
    return np.concatenate([r["x2"] for r in res.results], axis=0)



def declare_io_C(nc, scratch=None):
    io = {}
    io["x2i"] = (scratch["x2i"] if scratch and "x2i" in scratch else nc.dram_tensor("Cx2i", [TOK, D], F32, kind="ExternalInput").ap())
    io["srcT"] = nc.dram_tensor("CsrcT", [D, TOK], F32, kind="ExternalInput").ap()
    io["valT"] = nc.dram_tensor("CvalT", [D, TOK], F32, kind="ExternalInput").ap()
    io["wq"] = nc.dram_tensor("Cwq", [D, D], F32, kind="ExternalInput").ap()
    io["wk"] = nc.dram_tensor("Cwk", [D, D], F32, kind="ExternalInput").ap()
    io["wv"] = nc.dram_tensor("Cwv", [D, D], F32, kind="ExternalInput").ap()
    io["wo"] = nc.dram_tensor("Cwo", [D, D], F32, kind="ExternalInput").ap()
    io["bqr"] = nc.dram_tensor("Cbqr", [1, D], F32, kind="ExternalInput").ap()
    io["bkr"] = nc.dram_tensor("Cbkr", [1, D], F32, kind="ExternalInput").ap()
    io["bvr"] = nc.dram_tensor("Cbvr", [1, D], F32, kind="ExternalInput").ap()
    io["bor"] = nc.dram_tensor("Cbor", [1, D], F32, kind="ExternalInput").ap()
    io["lngr"] = nc.dram_tensor("Clngr", [1, D], F32, kind="ExternalInput").ap()
    io["lnbr"] = nc.dram_tensor("Clnbr", [1, D], F32, kind="ExternalInput").ap()
    io["x3o"] = (scratch["x3o"] if scratch and "x3o" in scratch else nc.dram_tensor("Cx3", [TOK, D], F32, kind="ExternalOutput").ap())
    return io


def emit_C(nc, st, p, io):
    x2i = io["x2i"]
    srcT = io["srcT"]
    valT = io["valT"]
    wq = io["wq"]
    wk = io["wk"]
    wv = io["wv"]
    wo = io["wo"]
    bqr = io["bqr"]
    bkr = io["bkr"]
    bvr = io["bvr"]
    bor = io["bor"]
    lngr = io["lngr"]
    lnbr = io["lnbr"]
    x3o = io["x3o"]
    sb = lambda name, shape, dt=BF16: st.enter_context(
        nc.sbuf_tensor("CC" + name, shape, dt))
    wbA = sb("wbA", [128, DC * D])
    wbB = sb("wbB", [128, DC * D])
    ident = sb("ident", [128, 128])
    onesT = sb("onesT", [128, 128])
    b_rows = sb("b_rows", [1, 4 * D])
    g_rows = sb("g_rows", [1, 2 * D])
    grepl = sb("grepl", [128, D])
    brepl = sb("brepl", [128, D])
    z_sb = sb("z_sb", [128, 1024])
    zline = sb("zline", [128, 1024])
    tmp_o = sb("tmp_o", [64, 1024])
    xt = [sb("xt0", [128, D], F32)] * 2
    lnx = [sb("lnx0", [128, D])] * 2
    ysb = [sb("ysb0", [128, D], F32)] * 2
    tmpf = sb("tmpf", [128, D], F32)
    sqb = sb("sqb", [128, D])
    qn_t = lnx[0]
    dzf = tmpf
    stats = st.enter_context(
        nc.sbuf_tensor("Cstats", [128, 48], F32))
    bigT_g = nc.sbuf_tensor("CbigT", [128, DC * TOK], BF16)
    bigT = bigT_g.__enter__()
    qnT_g = nc.sbuf_tensor("CqnT", [128, DC * TOK], BF16)
    qnT = qnT_g.__enter__()
    knT_g = nc.sbuf_tensor("CknT", [128, DC * TOK], BF16)
    knT = knT_g.__enter__()
    vnf_g = nc.sbuf_tensor("Cvn_f", [128, NT * NH * 65], BF16)
    vn_f = vnf_g.__enter__()
    ab_g = nc.sbuf_tensor("CAbuf", [128, 2048], BF16)
    u_g = nc.sbuf_tensor("Cu2048", [128, 2048], BF16)
    Abuf = ab_g.__enter__()
    u2048 = u_g.__enter__()
    Abuf2_pair = [Abuf, u2048]
    tminf = u2048[:, 0:1024]
    eqb = u2048[:, 1024:2048]
    sq_scr = Abuf[:, 0:1024]
    ps = lambda name, shape, dt=F32: st.enter_context(
        nc.psum_tensor("CC" + name, shape, dt))
    sce_guard = nc.psum_tensor("Csc_e", [128, 1024], F32)
    sc_e = sce_guard.__enter__()
    sco_guard = nc.psum_tensor("Csc_o", [128, 1024], F32)
    sc_o = sco_guard.__enter__()
    ptr_guard = nc.psum_tensor("Cp_tr", [128, 1024], BF16)
    p_tr = ptr_guard.__enter__()
    ptrB_guard = nc.psum_tensor("Cp_trB", [128, 1024], BF16)
    p_trB = ptrB_guard.__enter__()

    S = p.S
    s_ld = S.s_ld
    s_w = S.s_w
    s_g = S.s_g
    s_mm = S.s_mm
    s_v = S.s_v
    s_a = S.s_a
    s_st = S.s_st
    s_x = S.s_x
    s_ctx = S.s_ctx

    p.dma("sync", ident[:], nc.inline_tensor(IDENT_BF, "identC").ap(), s_w)
    consts_ready = s_w.now()
    p.dma("gpsimd", b_rows[:, 0 * D:1 * D], bqr, s_g)
    p.dma("gpsimd", b_rows[:, 1 * D:2 * D], bkr, s_g)
    p.dma("gpsimd", b_rows[:, 2 * D:3 * D], bvr, s_g)
    p.dma("gpsimd", b_rows[:, 3 * D:4 * D], bor, s_g)
    p.dma("gpsimd", g_rows[:, 0 * D:1 * D], lngr, s_g)
    p.dma("gpsimd", g_rows[:, 1 * D:2 * D], lnbr, s_g)
    p.dma("gpsimd", wbA[:].rearrange("p (c f) -> p c f", c=DC),
          wq.rearrange("(c p) f -> p c f", p=128), s_g)
    p.dma("gpsimd", wbB[:].rearrange("p (c f) -> p c f", c=DC),
          wk.rearrange("(c p) f -> p c f", p=128), s_g)
    w_ready = s_g.now()
    p.emit("vector", lambda v: v.memset(onesT[:], 1.0))
    for t in range(NT):
        p.emit("vector", lambda v, t=t: v.memset(
            vn_f[:, t * NH * 65:(t + 1) * NH * 65].rearrange(
                "p (h s) -> p h s", s=65)[:, :, 64:65], 1.0))
    ones_done = s_v.now()

    for which, dst in ((0, grepl), (1, brepl)):
        for g in range(2):
            p.emit("tensor", lambda te, which=which, g=g: te.matmul(
                sc_e[:, 512 * g:512 * (g + 1)], onesT[0:1, :],
                g_rows[:, which * D + 512 * g:which * D + 512 * (g + 1)],
                start=True, stop=True),
                wait=([ones_done, w_ready, (s_v, s_v.n)] if g == 0 else []),
                inc=(s_mm if g == 1 else None))
        p.emit("vector", lambda v, dst=dst: v.tensor_copy(
            dst[:], sc_e[:, 0:1024]), wait=[s_mm.now()])
    gb_done = s_v.now()

    def elu_norm_tile(psum_src, dst_tile, stats_g, w0=()):
        p.emit("vector", lambda v: v.tensor_scalar(
            tminf, psum_src[:, 0:1024], 0.0, None, ALU.min), wait=w0)
        p.emit("scalar", lambda s: s.activation(
            eqb, tminf, AF.Exp), wait=[s_v.now()])
        p.emit("vector", lambda v: v.scalar_tensor_tensor(
            eqb, eqb, -1.0, psum_src[:, 0:1024],
            ALU.add, ALU.max), wait=[s_a.now()])
        p.emit("vector", lambda v: v.tensor_tensor(
            sqb[:], eqb, eqb, ALU.mult))
        p.emit("vector", lambda v, stats_g=stats_g: v.reduce_sum(
            stats[:, stats_g:stats_g + 16],
            sqb[:].rearrange("p (h d) -> p h d", d=64), axis=AX.X))
        p.emit("vector", lambda v, stats_g=stats_g: v.tensor_scalar(
            stats[:, stats_g:stats_g + 16], stats[:, stats_g:stats_g + 16],
            1e-30, None, ALU.add))
        p.emit("scalar", lambda s, stats_g=stats_g: s.activation(
            stats[:, stats_g + 16:stats_g + 32],
            stats[:, stats_g:stats_g + 16], AF.Ln), wait=[s_v.now()])
        p.emit("scalar", lambda s, stats_g=stats_g: s.activation(
            stats[:, stats_g:stats_g + 16],
            stats[:, stats_g + 16:stats_g + 32], AF.Exp, scale=-0.5))
        qn_a = s_a.now()
        p.emit("vector", lambda v, stats_g=stats_g, dst_tile=dst_tile:
               v.tensor_tensor(
                   dst_tile[:, 0:1024].rearrange("p (h d) -> p h d", d=64),
                   u2048[:, 1024:2048].rearrange("p (h d) -> p h d", d=64),
                   stats[:, stats_g:stats_g + 16].unsqueeze(2).broadcast_to(
                       (128, 16, 64)),
                   ALU.mult),
               wait=[qn_a, (s_mm, s_mm.n)], inc=s_v)
        return [(s_v, s_v.n), (s_a, s_a.n)]

    for t in range(NT):
        b = t % 2
        xwait = [(s_v, s_v.n), (s_a, s_a.n)] if t >= 1 else []
        p.dma("sync", xt[b][:], x2i[128 * t:128 * (t + 1), :], s_ld[b],
              wait=xwait)
        ld = [s_ld[b].now()]
        m_ap, r_ap = _ln_stats(p, ld, xt[b][:], sq_scr, stats,
                               32 + 8 * (t % 2), s_v, s_a, LN_EPS, 1.0 / D)
        p.emit("vector", lambda v, b=b, m_ap=m_ap, r_ap=r_ap:
               v.tensor_scalar(lnx[b][:], xt[b][:], m_ap, r_ap,
                               ALU.subtract, ALU.mult),
               wait=[s_a.now()])
        bank2 = p_tr if t % 2 == 0 else p_trB
        tr_batch(p, ident, bank2,
                 [lnx[b][:, 128 * c:128 * (c + 1)] for c in range(DC)],
                 evacs=tr_views(bank2, bigT[:], DC, t, TOK),
                 evac_engines=("vector", "vector"),
                 first_wait=[(s_v, s_v.n)], extra_wait=[consts_ready])
    lnxT_done = s_v.now()

    def qkv_tile(t, w_sb, bias_off, src_fm, dst, fw=()):
        for half in range(2):
            for c in range(DC):
                p.emit("tensor", lambda te, t=t, c=c, half=half, dst=dst,
                       w_sb=w_sb, src_fm=src_fm: te.matmul(
                           dst[:, 512 * half:512 * (half + 1)],
                           src_fm[:, c * TOK + 128 * t:
                                  c * TOK + 128 * (t + 1)],
                           w_sb[:, c * D + 512 * half:
                                c * D + 512 * (half + 1)],
                           start=(c == 0), stop=False),
                       wait=(list(fw) if (half == 0 and c == 0) else []))
            p.emit("tensor", lambda te, half=half, bias_off=bias_off,
                   dst=dst: te.matmul(
                       dst[:, 512 * half:512 * (half + 1)], onesT[0:1, :],
                       b_rows[:, bias_off + 512 * half:
                              bias_off + 512 * (half + 1)],
                       start=False, stop=True),
                   inc=(s_mm if half == 1 else None))

    last_read = {}
    pend_tr = None
    for t in range(NT):
        dst = sc_e if t % 2 == 0 else sc_o
        fw = [(s_v, s_v.n), (s_a, s_a.n)] if t == 0 else []
        if t >= 2:
            fw = [(s_v, last_read[t - 2])]
        qkv_tile(t, wbA, 0, bigT, dst, fw)
        mmq = s_mm.now()
        if pend_tr is not None:
            pend_tr()
            pend_tr = None
        eld = elu_norm_tile(dst, qn_t, 0, [mmq])
        last_read[t] = s_v.n

        def mk_tr(t=t, eld=eld):
            def go():
                bank = p_tr if t % 2 == 0 else p_trB
                tr_batch(p, ident, bank,
                         [qn_t[:, 128 * c:128 * (c + 1)]
                          for c in range(DC)],
                         evacs=tr_views(bank, qnT[:], DC, t, TOK),
                         evac_engines=("vector", "vector"), first_wait=eld)
            return go
        pend_tr = mk_tr()
    pend_tr()
    q_done2 = tr_done(p)
    q_done = q_done2[0]
    q_mm_done = s_mm.now()

    p.dma("gpsimd", bigT[:].rearrange("p (c f) -> p c f", c=DC),
          srcT.rearrange("(c p) f -> p c f", p=128), s_g,
          wait=[q_mm_done])
    src_ready = s_g.now()
    last_read = {}
    pend_tr = None
    for t in range(NT):
        dst = sc_e if t % 2 == 0 else sc_o
        fw = [src_ready] if t == 0 else []
        if t == 1:
            fw = [(s_v, s_v.n)]
        if t >= 2:
            fw = [(s_v, last_read[t - 2])]
        qkv_tile(t, wbB, D, bigT, dst, fw)
        mmk = s_mm.now()
        if pend_tr is not None:
            pend_tr()
            pend_tr = None
        eld = elu_norm_tile(dst, qn_t, 0, [mmk])
        last_read[t] = s_v.n

        def mk_trk(t=t, eld=eld):
            def go():
                bank = p_tr if t % 2 == 0 else p_trB
                tr_batch(p, ident, bank,
                         [qn_t[:, 128 * c:128 * (c + 1)]
                          for c in range(DC)],
                         evacs=tr_views(bank, knT[:], DC, t, TOK),
                         evac_engines=("vector", "vector"), first_wait=eld)
            return go
        pend_tr = mk_trk()
    pend_tr()
    kn_done2 = tr_done(p)
    kn_done = kn_done2[0]
    kn_mm_done = s_mm.now()
    p.dma("gpsimd", wbA[:].rearrange("p (c f) -> p c f", c=DC),
          wv.rearrange("(c p) f -> p c f", p=128), s_g,
          wait=[q_mm_done])
    wv_ready = s_g.now()

    p.dma("gpsimd", bigT[:].rearrange("p (c f) -> p c f", c=DC),
          valT.rearrange("(c p) f -> p c f", p=128), s_g,
          wait=[kn_mm_done])
    val_ready = s_g.now()
    last_read = {}
    for t in range(NT):
        dst = sc_e if t % 2 == 0 else sc_o
        fw = [val_ready, wv_ready, ones_done, (s_v, s_v.n)] if t <= 1 else \
            [(s_v, last_read[t - 2])]
        qkv_tile(t, wbA, 2 * D, bigT, dst, fw)
        mmv = s_mm.now()
        p.emit("vector", lambda v, t=t, dst=dst: v.tensor_copy(
            vn_f[:, t * NH * 65:(t + 1) * NH * 65].rearrange(
                "p (h s) -> p h s", s=65)[:, :, 0:64],
            dst[:, 0:1024].rearrange("p (h d) -> p h d", d=64)),
            wait=[mmv])
        last_read[t] = s_v.n
    v_done = s_v.now()
    v_mm_done = s_mm.now()
    p.dma("gpsimd", wbB[:].rearrange("p (c f) -> p c f", c=DC),
          wo.rearrange("(c p) f -> p c f", p=128), s_g,
          wait=[kn_mm_done])
    wo_ready = s_g.now()

    ptrB_guard.__exit__(None, None, None)
    ptr_guard.__exit__(None, None, None)
    sco_guard.__exit__(None, None, None)
    sce_guard.__exit__(None, None, None)
    scA = st.enter_context(nc.psum_tensor("CscA", [128, 1024], F32))
    scB = st.enter_context(nc.psum_tensor("CscB", [128, 1024], F32))
    ctx_e = st.enter_context(nc.psum_tensor("Cctx_e", [65, 1024], F32))
    ctx_o = st.enter_context(nc.psum_tensor("Cctx_o", [65, 1024], F32))
    scP = [scA, scB]

    for hp in range(8):
        he, ho = 2 * hp, 2 * hp + 1
        for tqp in range(2):
            exp_done = {}
            av_done = {}
            it5 = 0

            def emit_av5(j):
                sc = j % 16
                tq2 = j // 16
                Ab = Abuf2_pair[j % 2]
                avw = [(s_a, exp_done[j])]
                for par, dstc, h in ((0, ctx_e, he), (64, ctx_o, ho)):
                    p.emit("tensor", lambda te, h=h, sc=sc, tq2=tq2,
                           par=par, dstc=dstc, Ab=Ab: te.matmul(
                               dstc[0:65, 512 * tq2:512 * (tq2 + 1)],
                               vn_f[:, sc * NH * 65 + 65 * h:
                                    sc * NH * 65 + 65 * (h + 1)],
                               Ab[:, 512 * (par // 64):
                                  512 * (par // 64) + 512],
                               start=(sc == 0), stop=(sc == 15)),
                           wait=(avw if par == 0 else []),
                           inc=(s_mm if par == 64 else None))
                av_done[j] = s_mm.n

            for tq2 in range(2):
                tq = 2 * tqp + tq2
                for sc in range(16):
                    dst = scP[it5 % 2]
                    sw = ([(s_v, s_v.n), (s_a, s_a.n), v_done] +
                          q_done2 + kn_done2)
                    if it5 >= 2:
                        sw = [(s_a, exp_done[it5 - 2])]
                    for par, h in ((0, he), (64, ho)):
                        p.emit("tensor", lambda te, h=h, sc=sc, tq=tq,
                               par=par, dst=dst: te.matmul(
                                   dst[:, 512 * (par // 64):
                                       512 * (par // 64) + 512],
                                   knT[par:par + 64,
                                       (h // 2) * TOK + 128 * sc:
                                       (h // 2) * TOK + 128 * (sc + 1)],
                                   qnT[par:par + 64,
                                       (h // 2) * TOK + 512 * tq:
                                       (h // 2) * TOK + 512 * (tq + 1)],
                                   start=True, stop=True,
                                   tile_position=(par, 0)),
                               wait=(sw if par == 0 else []),
                               inc=(s_mm if par == 64 else None))
                    scmm = s_mm.now()
                    if it5 >= 1:
                        emit_av5(it5 - 1)
                    Ab = Abuf2_pair[it5 % 2]
                    ew = [scmm]
                    if it5 >= 2:
                        ew.append((s_mm, av_done[it5 - 2]))
                    p.emit("scalar", lambda s, Ab=Ab, dst=dst: s.activation(
                        Ab[:, 0:1024], dst[:, 0:1024], AF.Exp,
                        scale=0.125), wait=ew)
                    exp_done[it5] = s_a.n
                    it5 += 1
            emit_av5(it5 - 1)
            ctx_mm = s_mm.now()
            for par, dstc, h, tag in ((0, ctx_e, he, "e"),
                                      (64, ctx_o, ho, "o")):
                p.emit("vector", lambda v, dstc=dstc: v.tensor_copy(
                    dzf[64:65, 0:1024], dstc[64:65, 0:1024]),
                    wait=[ctx_mm])

                def _recip2(v):
                    with nc.allow_low_precision(
                            reason="bf16 softmax denom within tolerance"):
                        return v.reciprocal(zline[64:65, 0:1024],
                                            dzf[64:65, 0:1024])
                p.emit("vector", _recip2)
                zl = s_v.now()
                for g in range(2):
                    p.emit("tensor", lambda te, g=g: te.matmul(
                        scA[0:65, 512 * g:512 * (g + 1)],
                        onesT[64:65, 0:65],
                        zline[64:65, 512 * g:512 * (g + 1)],
                        start=True, stop=True, tile_position=(64, 0)),
                        wait=([zl, (s_a, s_a.n)] if g == 0 else []),
                        inc=(s_mm if g == 1 else None))
                p.emit("vector", lambda v: v.tensor_copy(
                    z_sb[0:65, 0:1024], scA[0:65, 0:1024]),
                    wait=[s_mm.now()])
                if tag == "e":
                    p.emit("vector", lambda v, hp=hp, tqp=tqp, dstc=dstc:
                           v.tensor_tensor(
                               bigT[0:64, hp * TOK + 1024 * tqp:
                                    hp * TOK + 1024 * (tqp + 1)],
                               dstc[0:64, 0:1024], z_sb[0:64, 0:1024],
                               ALU.mult))
                else:
                    p.emit("vector", lambda v, dstc=dstc:
                           v.tensor_tensor(
                               tmp_o[0:64, 0:1024], dstc[0:64, 0:1024],
                               z_sb[0:64, 0:1024], ALU.mult),
                           wait=[(s_ctx, s_ctx.n)])
                    p.dma("sync", bigT[64:128, hp * TOK + 1024 * tqp:
                                       hp * TOK + 1024 * (tqp + 1)],
                          tmp_o[0:64, 0:1024], s_ctx,
                          wait=[(s_v, s_v.n)])
    ctx_done_v = s_v.now()
    ctx_done_dma = s_ctx.now()
    u_g.__exit__(None, None, None)
    ab_g.__exit__(None, None, None)
    vnf_g.__exit__(None, None, None)
    knT_g.__exit__(None, None, None)
    qnT_g.__exit__(None, None, None)
    p6_guards = []

    def _g6(name, shape, dt):
        g = nc.sbuf_tensor(name, shape, dt)
        p6_guards.append(g)
        return g.__enter__()
    xt6 = [_g6(f"Cxt6_{i}", [128, D], F32) for i in range(2)]
    ysb6 = [_g6(f"Cysb6_{i}", [128, D], F32) for i in range(2)]
    tmp6 = [_g6(f"Ctmp6_{i}", [128, D], F32) for i in range(2)]
    lnx6 = [_g6(f"Clnx6_{i}", [128, D], BF16) for i in range(2)]
    sq6 = _g6("Csq6", [128, D], BF16)

    store_done = {}
    resid_done = {}
    xatt_done = {}
    pend6 = None
    for t in range(NT):
        b = t % 2
        xw0 = [(s_a, s_a.n)]
        if t >= 2:
            xw0.append((s_v, resid_done[t - 2]))
        elif t == 0:
            xw0.append((s_v, s_v.n))
        p.dma("sync", xt6[b][:], x2i[128 * t:128 * (t + 1), :], s_ld[b],
              wait=xw0)
        xld = [s_ld[b].now()]
        pso = sc_e if t % 2 == 0 else sc_o
        for half in range(2):
            if half == 0:
                how = [wo_ready, gb_done]
                if t == 0:
                    how += [ctx_done_v, ctx_done_dma, (s_v, s_v.n),
                            (s_a, s_a.n)]
                elif t >= 2:
                    how.append((s_v, xatt_done[t - 2]))
            else:
                how = []
            for c in range(DC):
                p.emit("tensor", lambda te, t=t, c=c, half=half, pso=pso:
                       te.matmul(
                           pso[:, 512 * half:512 * (half + 1)],
                           bigT[:, c * TOK + 128 * t:
                                c * TOK + 128 * (t + 1)],
                           wbB[:, c * D + 512 * half:
                               c * D + 512 * (half + 1)],
                           start=(c == 0), stop=False),
                       wait=(how if c == 0 else []))
            p.emit("tensor", lambda te, half=half, pso=pso: te.matmul(
                pso[:, 512 * half:512 * (half + 1)], onesT[0:1, :],
                b_rows[:, 3 * D + 512 * half:3 * D + 512 * (half + 1)],
                start=False, stop=True), inc=s_mm)
        wo_mm = s_mm.now()
        if pend6 is not None:
            pend6()
            pend6 = None

        def mk_tail6(t=t, b=b, pso=pso, xld=xld, wo_mm=wo_mm):
            def go():
                m_ap, r_ap = _ln_stats(p, xld, xt6[b][:], sq6[:], stats,
                                       32 + 8 * (t % 2), s_v, s_a, LN_EPS,
                                       1.0 / D)
                p.emit("vector", lambda v, m_ap=m_ap, r_ap=r_ap:
                       v.tensor_scalar(lnx6[b][:], xt6[b][:], m_ap, r_ap,
                                       ALU.subtract, ALU.mult),
                       wait=[s_a.now()])
                for half in range(2):
                    xw = [wo_mm]
                    if t >= 2 and half == 0:
                        xw.append((s_st[t % 2], store_done[t - 2]))
                    p.emit("vector", lambda v, half=half: v.tensor_tensor(
                        ysb6[b][:, 512 * half:512 * (half + 1)],
                        pso[:, 512 * half:512 * (half + 1)],
                        lnx6[b][:, 512 * half:512 * (half + 1)], ALU.add),
                        wait=xw)
                xatt_done[t] = s_v.n
                x_att = s_v.now()
                m_ap, r_ap = _ln_stats(p, [x_att], ysb6[b][:], sq6[:], stats,
                                       32 + 8 * (t % 2), s_v, s_a,
                                       L1_LN_EPS, 1.0 / D)
                p.emit("vector", lambda v, m_ap=m_ap, r_ap=r_ap:
                       v.tensor_scalar(tmp6[b][:], ysb6[b][:], m_ap, r_ap,
                                       ALU.subtract, ALU.mult),
                       wait=[s_a.now()])
                p.emit("vector", lambda v: v.tensor_tensor(
                    tmp6[b][:], tmp6[b][:], grepl[:], ALU.mult))
                p.emit("vector", lambda v: v.tensor_tensor(
                    tmp6[b][:], tmp6[b][:], brepl[:], ALU.add))
                hln = s_v.now()
                m_ap, r_ap = _ln_stats(p, [hln], tmp6[b][:], sq6[:], stats,
                                       32 + 8 * (t % 2), s_v, s_a, LN_EPS,
                                       1.0 / D)
                p.emit("vector", lambda v, m_ap=m_ap, r_ap=r_ap:
                       v.tensor_scalar(ysb6[b][:], tmp6[b][:], m_ap, r_ap,
                                       ALU.subtract, ALU.mult),
                       wait=[s_a.now()])
                p.emit("vector", lambda v: v.tensor_tensor(
                    ysb6[b][:], ysb6[b][:], xt6[b][:], ALU.add))
                resid_done[t] = s_v.n
                p.dma("sync", x3o[128 * t:128 * (t + 1), :], ysb6[b][:],
                      s_st[b], wait=[s_v.now()])
                store_done[t] = s_st[b].n
            return go
        pend6 = mk_tail6()
    pend6()
    p.emit("sync", lambda sy: sy.nop(),
           wait=[s_st[0].now(), s_st[1].now()])
    for g in reversed(p6_guards):
        g.__exit__(None, None, None)
    bigT_g.__exit__(None, None, None)


def build_C():
    nc = bass.Bass("TRN2", target_bir_lowering=False, debug=False)
    io = declare_io_C(nc)
    st = contextlib.ExitStack()
    with st:
        p = Prog(nc, st)
        make_sems(p)
        emit_C(nc, st, p, io)
        p.finalize()
    return nc


def _prep_C(inputs, core, x2=None):
    t0 = core * TOK
    d = {
        "srcT": np.ascontiguousarray(
            np.asarray(inputs["L1_attn_source_embedding"]).T),
        "valT": np.ascontiguousarray(
            np.asarray(inputs["L1_attn_value_embedding"]).T),
        "wq": np.ascontiguousarray(np.asarray(inputs["L1_attn_Wq"])),
        "wk": np.ascontiguousarray(np.asarray(inputs["L1_attn_Wk"])),
        "wv": np.ascontiguousarray(np.asarray(inputs["L1_attn_Wv"])),
        "wo": np.ascontiguousarray(np.asarray(inputs["L1_attn_Wo"])),
        "bqr": np.asarray(inputs["L1_attn_bq"]).reshape(1, D),
        "bkr": np.asarray(inputs["L1_attn_bk"]).reshape(1, D),
        "bvr": np.asarray(inputs["L1_attn_bv"]).reshape(1, D),
        "bor": np.asarray(inputs["L1_attn_bo"]).reshape(1, D),
        "lngr": np.asarray(inputs["L1_attn_ln_g"]).reshape(1, D),
        "lnbr": np.asarray(inputs["L1_attn_ln_b"]).reshape(1, D),
    }
    if x2 is not None:
        d["x2i"] = np.ascontiguousarray(x2[t0:t0 + TOK])
    return d


def run_C(inputs, x2):
    nc = build_C()
    in_maps = [{"C" + k: v for k, v in _prep_C(inputs, c, x2).items()} for c in range(NCORES)]
    res = run_bass_kernel_spmd(nc, in_maps, core_ids=list(range(NCORES)))
    return np.concatenate([r["x3"] for r in res.results], axis=0)



def declare_io_D(nc, scratch=None):
    io = {}
    io["x3i"] = (scratch["x3i"] if scratch and "x3i" in scratch else nc.dram_tensor("Dx3i", [TOK, D], F32, kind="ExternalInput").ap())
    io["wgT"] = nc.dram_tensor("DwgT", [D, 3072], I32, kind="ExternalInput").ap()
    io["wuT"] = nc.dram_tensor("DwuT", [D, 3072], I32, kind="ExternalInput").ap()
    io["wdT"] = nc.dram_tensor("DwdT", [3072, D], I32, kind="ExternalInput").ap()
    io["x4o"] = (scratch["x4o"] if scratch and "x4o" in scratch else nc.dram_tensor("Dx4", [TOK, D], F32, kind="ExternalOutput").ap())
    return io


def emit_D(nc, st, p, io):
    x3i = io["x3i"]
    wgT = io["wgT"]
    wuT = io["wuT"]
    wdT = io["wdT"]
    x4o = io["x4o"]
    sb = lambda name, shape, dt=BF16: st.enter_context(
        nc.sbuf_tensor("DD" + name, shape, dt))
    wg_s = sb("wg_s", [128, DC * 3072])
    wu_s = sb("wu_s", [128, DC * 3072])
    wd_s = sb("wd_s", [128, 24 * D])
    ident = sb("ident", [128, 128])
    lnxT_t = [sb("lnxT_t0", [128, DC * 128]), sb("lnxT_t1", [128, DC * 128])]
    gate_s2 = [sb("gate_s0", [128, 1024]), sb("gate_s1", [128, 1024])]
    h_bf2 = [sb("h_bf0", [128, 1024]), sb("h_bf1", [128, 1024])]
    hT_sb2 = [sb("hT_sb0", [128, 1024]), sb("hT_sb1", [128, 1024])]
    xt = [sb(f"xt{i}", [128, D], F32) for i in range(2)]
    lnx = [sb(f"lnx{i}", [128, D]) for i in range(2)]
    ysb = [sb(f"ysb{i}", [128, D], F32) for i in range(2)]
    sq_scr = sb("sq_scr", [128, 1536])
    stats = sb("stats", [128, 16], F32)
    ps = lambda name, shape, dt=F32: st.enter_context(
        nc.psum_tensor("DD" + name, shape, dt))
    psum_gA = ps("psum_gA", [128, 1024])
    psum_gB = ps("psum_gB", [128, 1024])
    psum_y = ps("psum_y", [128, 1024])
    p_trA = ps("p_trA", [128, 1024], BF16)
    p_trB = ps("p_trB", [128, 1024], BF16)

    S = p.S
    s_ld = S.s_ld
    s_w = S.s_w
    s_g = S.s_g
    s_mm = S.s_mm
    s_v = S.s_v
    s_a = S.s_a
    s_st = S.s_st

    p.dma("sync", ident[:], nc.inline_tensor(IDENT_BF, "identD").ap(), s_w)
    consts_ready = s_w.now()
    p.dma("gpsimd", wg_s[:].rearrange("p (c f) -> p c f", c=DC),
          wgT.rearrange("(c p) f -> p c f", p=128), s_g)
    p.dma("gpsimd", wu_s[:].rearrange("p (c f) -> p c f", c=DC),
          wuT.rearrange("(c p) f -> p c f", p=128), s_g)
    for blk in range(3):
        p.dma("gpsimd",
              wd_s[:, blk * 8 * D:(blk + 1) * 8 * D].rearrange(
                  "p (c f) -> p c f", c=8),
              wdT[blk * 1024:(blk + 1) * 1024, :].rearrange(
                  "(c p) f -> p c f", p=128), s_g)
    w_ready = s_g.now()


    store_done = {}
    y_evac = 0
    norm_done = {}
    upmm_done = {}
    war = {}
    last_silu = [None]
    last_stt = [None]
    last_stt_same = {}
    last_trs_same = {}
    pend_body = None
    for t in range(NT):
        b = t % 2
        xwait = [(s_v, norm_done[t - 2]), (s_a, s_a.n)] if t >= 2 else []
        p.dma("sync", xt[b][:], x3i[128 * t:128 * (t + 1), :], s_ld[b],
              wait=xwait)
        ld = [s_ld[b].now()]
        if pend_body is not None:
            pend_body()
            pend_body = None
        m_ap, r_ap = _ln_stats(p, ld, xt[b][:], sq_scr[:, 0:1024], stats,
                               8 * (t % 2), s_v, s_a, LN_EPS, 1.0 / D,
                               extra_r_scale=1.0 / 127.0)
        stats_a = s_a.now()

        def mk_body(t=t, b=b, m_ap=m_ap, r_ap=r_ap, stats_a=stats_a):
          def go():
            nonlocal y_evac
            p.emit("vector", lambda v: v.tensor_scalar(
                lnx[b][:], xt[b][:], m_ap, r_ap,
                ALU.subtract, ALU.mult), wait=[stats_a])
            norm_done[t] = s_v.n
            trw = [consts_ready]
            if t >= 2:
                trw.append((s_mm, upmm_done[t - 2]))
            tr_rotate(p, ident, [p_trA, p_trB],
                      [(lnx[b][:, 128 * c:128 * (c + 1)],
                        lnxT_t[b][:, 128 * c:128 * (c + 1)]) for c in range(DC)],
                      extra_wait=trw,
                      first_wait=[(s_v, norm_done[t])])
            lnxT_ready2 = tr_done(p)
            y_first = True
            for third in range(3):
                par = third % 2
                gate_s = gate_s2[par]
                h_bf = h_bf2[par]
                hT_sb = hT_sb2[par]
                gw = lnxT_ready2 + [w_ready]
                if last_silu[0] is not None:
                    gw.append((s_a, last_silu[0]))
                for g in range(2):
                    for e in range(DC):
                        p.emit("tensor", lambda te, b=b, third=third, g=g, e=e:
                               te.matmul(
                                   psum_gA[:, 512 * g:512 * (g + 1)],
                                   lnxT_t[b][:, 128 * e:128 * (e + 1)],
                                   wg_s[:, e * 3072 + 1024 * third + 512 * g:
                                        e * 3072 + 1024 * third + 512 * (g + 1)],
                                   start=(e == 0), stop=(e == DC - 1)),
                               wait=(gw if (g == 0 and e == 0) else []),
                               inc=(s_mm if (g == 1 and e == DC - 1) else None))
                gmm = s_mm.now()
                uw = []
                if last_stt[0] is not None:
                    uw.append((s_v, last_stt[0]))
                for g in range(2):
                    for e in range(DC):
                        p.emit("tensor", lambda te, b=b, third=third, g=g, e=e:
                               te.matmul(
                                   psum_gB[:, 512 * g:512 * (g + 1)],
                                   lnxT_t[b][:, 128 * e:128 * (e + 1)],
                                   wu_s[:, e * 3072 + 1024 * third + 512 * g:
                                        e * 3072 + 1024 * third + 512 * (g + 1)],
                                   start=(e == 0), stop=(e == DC - 1)),
                               wait=(uw if (g == 0 and e == 0) else []),
                               inc=(s_mm if (g == 1 and e == DC - 1) else None))
                umm = s_mm.now()
                sw2 = [gmm]
                if (t, third, "stt") in war or True:
                    pass
                if last_stt_same.get(par) is not None:
                    sw2.append((s_v, last_stt_same[par]))
                p.emit("scalar", lambda s, gate_s=gate_s: s.activation(
                    gate_s[:], psum_gA[:], AF.Silu), wait=sw2)
                last_silu[0] = s_a.n
                hww = [umm, (s_a, last_silu[0])]
                if last_trs_same.get(par) is not None:
                    hww.append((s_mm, last_trs_same[par]))
                p.emit("vector", lambda v, gate_s=gate_s, h_bf=h_bf:
                       v.scalar_tensor_tensor(
                           h_bf[:], gate_s[:], 1.0 / 127.0, psum_gB[:],
                           ALU.mult, ALU.mult), wait=hww)
                last_stt[0] = s_v.n
                last_stt_same[par] = s_v.n
                tr_rotate(p, ident, [p_trA, p_trB],
                          [(h_bf[:, 128 * hc:128 * (hc + 1)],
                            hT_sb[:, 128 * hc:128 * (hc + 1)])
                           for hc in range(DC)],
                          first_wait=[(s_v, last_stt[0])])
                last_trs_same[par] = s_mm.n
                hT_ready2 = tr_done(p)
                for hc in range(DC):
                    for g in range(2):
                        dw = []
                        if hc == 0 and g == 0:
                            dw = list(hT_ready2)
                            if y_first and t > 0:
                                dw.append((s_v, y_evac))
                        p.emit("tensor", lambda te, third=third, hc=hc, g=g,
                               y_first=y_first, hT_sb=hT_sb: te.matmul(
                                   psum_y[:, 512 * g:512 * (g + 1)],
                                   hT_sb[:, 128 * hc:128 * (hc + 1)],
                                   wd_s[:, (8 * third + hc) * D + 512 * g:
                                        (8 * third + hc) * D + 512 * (g + 1)],
                                   start=(y_first and hc == 0),
                                   stop=(third == 2 and hc == DC - 1)),
                               wait=dw,
                               inc=(s_mm if (third == 2 and hc == DC - 1 and
                                             g == 1) else None))
                y_first = False
            upmm_done[t] = s_mm.n
            ymm = s_mm.now()
            hw0 = [ymm]
            if t >= 2:
                hw0.append((s_st[t % 2], store_done[t - 2]))
            p.emit("vector", lambda v, b=b: v.tensor_copy(
                ysb[b][:], psum_y[:]), wait=hw0)
            y_evac = s_v.n
            m_ap, r_ap = _ln_stats(p, [s_v.now()], ysb[b][:],
                                   sq_scr[:, 0:1024], stats, 8 * (t % 2),
                                   s_v, s_a, LN_EPS, 1.0 / D)
            p.emit("vector", lambda v, b=b, m_ap=m_ap, r_ap=r_ap:
                   v.tensor_scalar(ysb[b][:], ysb[b][:], m_ap, r_ap,
                                   ALU.subtract, ALU.mult), wait=[s_a.now()])
            p.emit("vector", lambda v, b=b: v.tensor_tensor(
                ysb[b][:], ysb[b][:], xt[b][:], ALU.add))
            p.dma("sync", x4o[128 * t:128 * (t + 1), :], ysb[b][:], s_st[b],
                  wait=[s_v.now()])
            store_done[t] = s_st[b].n
          return go
        pend_body = mk_body()
    pend_body()
    p.emit("sync", lambda sy: sy.nop(),
           wait=[s_st[0].now(), s_st[1].now()])


def build_D():
    nc = bass.Bass("TRN2", target_bir_lowering=False, debug=False)
    io = declare_io_D(nc)
    st = contextlib.ExitStack()
    with st:
        p = Prog(nc, st)
        make_sems(p)
        emit_D(nc, st, p, io)
        p.finalize()
    return nc


def _prep_D(inputs, core, x3=None):
    t0 = core * TOK
    d = {
        "wgT": np.ascontiguousarray(np.asarray(inputs["L1_ffn_w_gate"]).T),
        "wuT": np.ascontiguousarray(np.asarray(inputs["L1_ffn_w_up"]).T),
        "wdT": np.ascontiguousarray(np.asarray(inputs["L1_ffn_w_down"]).T),
    }
    if x3 is not None:
        d["x3i"] = np.ascontiguousarray(x3[t0:t0 + TOK])
    return d


def run_D(inputs, x3):
    nc = build_D()
    in_maps = [{"D" + k: v for k, v in _prep_D(inputs, c, x3).items()} for c in range(NCORES)]
    res = run_bass_kernel_spmd(nc, in_maps, core_ids=list(range(NCORES)))
    return np.concatenate([r["x4"] for r in res.results], axis=0)


def build_full():
    nc = bass.Bass("TRN2", target_bir_lowering=False, debug=False)
    x1s = nc.dram_tensor("x1s", [TOK, D], F32, kind="Internal").ap()
    x2s = nc.dram_tensor("x2s", [TOK, D], F32, kind="Internal").ap()
    x3s = nc.dram_tensor("x3s", [TOK, D], F32, kind="Internal").ap()
    ios = [
        declare_io_A(nc, scratch={"x1o": x1s}),
        declare_io_B(nc, scratch={"x1i": x1s, "x2o": x2s}),
        declare_io_C(nc, scratch={"x2i": x2s, "x3o": x3s}),
        declare_io_D(nc, scratch={"x3i": x3s}),
    ]
    emits = [emit_A, emit_B, emit_C, emit_D]
    with contextlib.ExitStack() as st0:
        p = Prog(nc, st0)
        make_sems(p)
        for i, (em, io) in enumerate(zip(emits, ios)):
            with contextlib.ExitStack() as stX:
                em(nc, stX, p, io)
            if i < 3:
                barrier(p)
        p.finalize()
    return nc


def kernel(**inputs):
    nc = build_full()
    in_maps = []
    for c in range(NCORES):
        m = {}
        for X, d in (("A", _prep_A(inputs, c)), ("B", _prep_B(inputs, c)),
                     ("C", _prep_C(inputs, c)), ("D", _prep_D(inputs, c))):
            for k, v in d.items():
                m[X + k] = v
        in_maps.append(m)
    res = run_bass_kernel_spmd(nc, in_maps, core_ids=list(range(NCORES)))
    x4 = np.concatenate([r["Dx4"] for r in res.results], axis=0)
    return x4.reshape(4, LBATCH, D)


def kernel_multilaunch(**inputs):
    x1 = run_A(inputs)
    x2 = run_B(inputs, x1)
    x3 = run_C(inputs, x2)
    x4 = run_D(inputs, x3)
    return x4.reshape(4, LBATCH, D)


# version 20 (final)
# speedup vs baseline: 1.1885x; 1.1885x over previous
import math
import contextlib
import numpy as np
import ml_dtypes
import concourse.bass as bass
import concourse.mybir as mybir
from concourse.bass_utils import run_bass_kernel_spmd

F32 = mybir.dt.float32
BF16 = mybir.dt.bfloat16
I32 = mybir.dt.int32
AF = mybir.ActivationFunctionType
ALU = mybir.AluOpType
AX = mybir.AxisListType
BF = ml_dtypes.bfloat16

NCORES = 8
TOK = 2048
NT = TOK // 128
D = 1024
DC = D // 128
NH = 16
HD = 64
LBATCH = 4096
NBT = LBATCH // 128
LN_EPS = 1e-5
COS_EPS = 1e-4
L1_LN_EPS = 1e-12



class Sem:
    def __init__(self, nc, stack, name):
        self.h = stack.enter_context(nc.semaphore(name))
        self.n = 0

    def now(self):
        return (self, self.n)


class Prog:

    ENGINES = ("sync", "gpsimd", "tensor", "vector", "scalar")

    def __init__(self, nc, stack):
        self.nc = nc
        self.stack = stack
        self.stream = {e: [] for e in self.ENGINES}
        self.esem = {}

    def sem(self, name):
        return Sem(self.nc, self.stack, name)

    def emit(self, engine, fn, wait=(), inc=None, inc_k=1):
        wait = list(wait)
        es = self.esem.get(engine)
        if es is not None:
            wait.append(es.now())
            if inc is None:
                inc = es
        if inc is not None:
            inc.n += inc_k
        self.stream[engine].append((fn, tuple(wait), inc, inc_k))

    def dma(self, queue, out, in_, sem, wait=()):
        wait = list(wait)
        if queue == "gpsimd":
            wait.append(sem.now())
        self.emit(queue, lambda g, o=out, i=in_: g.dma_start(out=o, in_=i),
                  wait=wait, inc=sem, inc_k=16)

    def finalize(self):
        with self.nc.Block() as block:
            for eng_name in self.ENGINES:
                entries = self.stream[eng_name]
                if not entries:
                    continue

                def make(entries):
                    def f(eng):
                        for fn, wait, inc, k in entries:
                            for s, v in wait:
                                if v > 0:
                                    eng.wait_ge(s.h, v)
                            inst = fn(eng)
                            if inc is not None:
                                inst.then_inc(inc.h, k)
                    return f

                getattr(block, eng_name)(make(entries))


def _ln_stats(p, ld_wait, src_ap, sq_scr, stats, g0, v_sem, a_sem, eps, inv_n,
              extra_r_scale=None):
    c = lambda i, w=1: stats[:, g0 + i:g0 + i + w]
    p.emit("vector", lambda v: v.reduce_sum(c(0), src_ap, axis=AX.X),
           wait=ld_wait, inc=v_sem)
    p.emit("scalar", lambda s: s.activation(sq_scr, src_ap, AF.Square,
                                            accum_out=c(1)),
           wait=ld_wait, inc=a_sem)
    p.emit("vector", lambda v: v.tensor_scalar(c(2, 2), c(0, 2), 1.0 / 1024,
                                               None, ALU.mult),
           wait=[a_sem.now()], inc=v_sem)
    p.emit("vector", lambda v: v.tensor_tensor(c(4, 2), c(2, 2), c(2, 2),
                                               ALU.mult), inc=v_sem)
    p.emit("vector", lambda v: v.tensor_tensor(c(6, 2), c(3, 2), c(4, 2),
                                               ALU.subtract), inc=v_sem)
    p.emit("vector", lambda v: v.tensor_scalar(c(0, 2), c(6, 2), 0.0,
                                               float(eps), ALU.max,
                                               ALU.add), inc=v_sem)
    p.emit("scalar", lambda s: s.activation(c(4, 2), c(0, 2), AF.Ln),
           wait=[v_sem.now()], inc=a_sem)
    p.emit("scalar", lambda s: s.activation(c(6, 2), c(4, 2), AF.Exp,
                                            scale=-0.5), inc=a_sem)
    if extra_r_scale is not None:
        p.emit("vector", lambda v: v.tensor_scalar(
            c(6, 2), c(6, 2), float(extra_r_scale), None, ALU.mult),
            wait=[a_sem.now()], inc=v_sem)
    return c(2), c(6)



class SemBundle:
    pass


def make_sems(p):
    S = SemBundle()
    S.s_ld = [p.sem("s_ld0"), p.sem("s_ld1")]
    S.s_w = p.sem("s_w")
    S.s_g = p.sem("s_g")
    S.s_mm = p.sem("s_mm")
    S.s_v = p.sem("s_v")
    S.s_a = p.sem("s_a")
    S.s_st = [p.sem("s_st0"), p.sem("s_st1")]
    S.s_x = p.sem("s_x")
    S.s_kv = p.sem("s_kv")
    S.s_ctx = p.sem("s_ctx")
    p.S = S
    p.esem = {"vector": S.s_v, "scalar": S.s_a}
    return S


def barrier(p):
    S = p.S
    snap = [s.now() for s in
            [S.s_ld[0], S.s_ld[1], S.s_w, S.s_g, S.s_mm, S.s_v, S.s_a,
             S.s_st[0], S.s_st[1], S.s_x, S.s_kv, S.s_ctx] if s.n > 0]
    for eng in Prog.ENGINES:
        p.emit(eng, lambda e: e.nop(), wait=snap)



def _sincos_tables():
    l = np.arange(LBATCH, dtype=np.float64)
    ang = (np.pi / 2) * (l + 1.0) / LBATCH
    sin = np.sin(ang).astype(np.float32).reshape(NBT, 128).T.copy()
    cos = np.cos(ang).astype(np.float32).reshape(NBT, 128).T.copy()
    return sin, cos


IDENT_BF = np.eye(128, dtype=BF)

KV_SLOT = []
for _h in range(NH):
    _b = 0 if _h < 5 else (1 if _h < 10 else 2)
    KV_SLOT.append((_b, 512 * _b + 65 * (_h - (0, 5, 10)[_b])))
KV_BANK_FIRST = {0: 0, 1: 5, 2: 10}
KV_BANK_LAST = {0: 4, 1: 9, 2: 15}


def tr_rotate(p, ident, bufs, pairs, extra_wait=(), first_wait=None,
              alt_evac=True):
    S = p.S
    if not hasattr(p, "_tr_last"):
        p._tr_last = {}
    evn = {}
    eng_of = {}
    nb = len(bufs)
    for c, (sap, dap) in enumerate(pairs):
        buf = bufs[c % nb]
        if c == 0:
            w = (list(first_wait) if first_wait is not None
                 else [(S.s_v, S.s_v.n)])
            w += list(extra_wait)
        else:
            w = []
        if c >= nb:
            sem = S.s_v if eng_of[c - nb] == "vector" else S.s_a
            w.append((sem, evn[c - nb]))
        elif id(buf) in p._tr_last:
            last = p._tr_last[id(buf)]
            w.extend(last if isinstance(last, list) else [last])
        p.emit("tensor", lambda te, sap=sap, buf=buf: te.transpose(
            buf[:, 0:128], sap, ident[:]), wait=w, inc=S.s_mm)
        eng = "vector" if (not alt_evac or c % 2 == 0) else "scalar"
        if eng == "vector":
            p.emit("vector", lambda v, dap=dap, buf=buf: v.tensor_copy(
                dap, buf[:, 0:128]), wait=[S.s_mm.now()])
            evn[c] = S.s_v.n
            p._tr_last[id(buf)] = (S.s_v, S.s_v.n)
        else:
            p.emit("scalar", lambda s, dap=dap, buf=buf: s.activation(
                dap, buf[:, 0:128], AF.Copy), wait=[S.s_mm.now()])
            evn[c] = S.s_a.n
            p._tr_last[id(buf)] = (S.s_a, S.s_a.n)
        eng_of[c] = eng


def tr_done(p):
    S = p.S
    return [(S.s_v, S.s_v.n), (S.s_a, S.s_a.n)]


def tr_batch(p, ident, bank, srcs, contig_dst=None, first_wait=None,
             extra_wait=(), evacs=None, evac_engines=("vector", "scalar")):
    S = p.S
    if not hasattr(p, "_tr_last"):
        p._tr_last = {}
    n = len(srcs)
    w0 = (list(first_wait) if first_wait is not None
          else [(S.s_v, S.s_v.n), (S.s_a, S.s_a.n)])
    w0 += list(extra_wait)
    if id(bank) in p._tr_last:
        last = p._tr_last[id(bank)]
        w0 += last if isinstance(last, list) else [last]
    for i, sap in enumerate(srcs):
        p.emit("tensor", lambda te, sap=sap, i=i: te.matmul(
            bank[:, 128 * i:128 * (i + 1)], sap, ident[:],
            start=(i == 0), stop=(i == n - 1), is_transpose=True),
            wait=(w0 if i == 0 else []),
            inc=(S.s_mm if i == n - 1 else None))
    mmdone = S.s_mm.now()
    if evacs is None:
        ne = len(evac_engines)
        per = (n * 128) // ne
        evacs = [(contig_dst[:, per * j:per * (j + 1)],
                  bank[:, per * j:per * (j + 1)]) for j in range(ne)]
    marks = []
    for j, (dap, sap) in enumerate(evacs):
        eng = evac_engines[j % len(evac_engines)]
        if eng == "vector":
            p.emit("vector", lambda v, dap=dap, sap=sap: v.tensor_copy(
                dap, sap), wait=[mmdone])
            marks.append((S.s_v, S.s_v.n))
        else:
            p.emit("scalar", lambda s, dap=dap, sap=sap: s.activation(
                dap, sap, AF.Copy), wait=[mmdone])
            marks.append((S.s_a, S.s_a.n))
    p._tr_last[id(bank)] = marks
    return marks


def tr_views(bank, dst_fm, n, t, stride):
    dv = dst_fm.rearrange("p (c f) -> p c f", f=stride)
    sv = bank[:, 0:n * 128].rearrange("p (c f) -> p c f", f=128)
    h = n // 2
    return [(dv[:, 0:h, 128 * t:128 * (t + 1)], sv[:, 0:h, :]),
            (dv[:, h:n, 128 * t:128 * (t + 1)], sv[:, h:n, :])]



def declare_io_A(nc, scratch=None):
    io = {}
    io["xb"] = nc.dram_tensor("Axb", [LBATCH, D], F32, kind="ExternalInput").ap()
    io["wqt"] = nc.dram_tensor("Awqt", [D, D], F32, kind="ExternalInput").ap()
    io["wkt"] = nc.dram_tensor("Awkt", [D, D], F32, kind="ExternalInput").ap()
    io["wvt"] = nc.dram_tensor("Awvt", [D, D], F32, kind="ExternalInput").ap()
    io["wot"] = nc.dram_tensor("Awot", [D, D], F32, kind="ExternalInput").ap()
    io["bqpp"] = nc.dram_tensor("Abqpp", [128, DC], F32, kind="ExternalInput").ap()
    io["bkr"] = nc.dram_tensor("Abkr", [1, D], F32, kind="ExternalInput").ap()
    io["bvr"] = nc.dram_tensor("Abvr", [1, D], F32, kind="ExternalInput").ap()
    io["bor"] = nc.dram_tensor("Abor", [1, D], F32, kind="ExternalInput").ap()
    io["sinp"] = nc.dram_tensor("Asinp", [128, NBT], F32, kind="ExternalInput").ap()
    io["cosp"] = nc.dram_tensor("Acosp", [128, NBT], F32, kind="ExternalInput").ap()
    io["x1o"] = (scratch["x1o"] if scratch and "x1o" in scratch else nc.dram_tensor("Ax1", [TOK, D], F32, kind="ExternalOutput").ap())
    return io


def emit_A(nc, st, p, io):
    xb = io["xb"]
    wqt = io["wqt"]
    wkt = io["wkt"]
    wvt = io["wvt"]
    wot = io["wot"]
    bqpp = io["bqpp"]
    bkr = io["bkr"]
    bvr = io["bvr"]
    bor = io["bor"]
    sinp = io["sinp"]
    cosp = io["cosp"]
    x1o = io["x1o"]
    sb = lambda name, shape, dt=BF16: st.enter_context(
        nc.sbuf_tensor("AA" + name, shape, dt))
    lnxT = sb("lnxT", [128, DC * LBATCH])
    qfm = sb("qfm", [128, DC * TOK])
    wA = sb("wA", [128, DC * D])
    wB = sb("wB", [128, DC * D])
    kvs = sb("kvs", [128, NH * 65])
    kvw = sb("kvw", [128, NH * 65])
    ident = sb("ident", [128, 128])
    ones1 = sb("ones1", [1, 128])
    bq_s = sb("bq_s", [128, DC], F32)
    bk_s = sb("bk_s", [1, D])
    bv_s = sb("bv_s", [1, D])
    bo_s = sb("bo_s", [1, D])
    sin_s = sb("sin_s", [128, NBT], F32)
    cos_s = sb("cos_s", [128, NBT], F32)
    xt = [sb(f"xt{i}", [128, D], F32) for i in range(2)]
    lnx = [sb(f"lnx{i}", [128, D]) for i in range(2)]
    ktl = [sb(f"ktl{i}", [128, NH * 128]) for i in range(2)]
    vtl = [sb(f"vtl{i}", [128, NH * 65]) for i in range(2)]
    acmb = [sb(f"acmb{i}", [128, NH * 65], F32) for i in range(2)]
    abf = [sb(f"abf{i}", [128, D]) for i in range(2)]
    atT = [sb(f"atT{i}", [128, D]) for i in range(2)]
    hsb = [sb(f"hsb{i}", [128, D], F32) for i in range(2)]
    dens = [sb("dens0", [128, NH], F32), sb("dens1", [128, NH], F32)]
    zrec = [sb("zrec0", [128, NH], F32), sb("zrec1", [128, NH], F32)]
    sq_scr = sb("sq_scr", [128, D])
    stats = sb("stats", [128, 16], F32)
    ps = lambda name, shape, dt=F32: st.enter_context(
        nc.psum_tensor("AA" + name, shape, dt))
    psum_kv = ps("psum_kv", [128, 1536])
    psum_q = ps("psum_q", [128, 1024])
    p_tr = ps("p_tr", [128, 1024], BF16)
    ptrB_guard = nc.psum_tensor("Ap_trB", [128, 1024], BF16)
    p_trB = ptrB_guard.__enter__()

    S = p.S
    s_ld = S.s_ld
    s_w = S.s_w
    s_g = S.s_g
    s_kv = S.s_kv
    s_mm = S.s_mm
    s_v = S.s_v
    s_a = S.s_a
    s_st = S.s_st
    s_x = S.s_x

    p.dma("sync", ident[:], nc.inline_tensor(IDENT_BF, "identA").ap(), s_w)
    p.dma("sync", bq_s[:], bqpp, s_w)
    p.dma("sync", sin_s[:], sinp, s_w)
    p.dma("sync", cos_s[:], cosp, s_w)
    ident_ready = s_w.now()
    p.dma("gpsimd", bk_s[:], bkr, s_g)
    p.dma("gpsimd", bv_s[:], bvr, s_g)
    p.dma("gpsimd", bo_s[:], bor, s_g)
    p.dma("gpsimd", wA[:].rearrange("p (c f) -> p c f", c=DC),
          wqt.rearrange("(c p) f -> p c f", p=128), s_g)
    p.dma("gpsimd", wB[:].rearrange("p (c f) -> p c f", c=DC),
          wkt.rearrange("(c p) f -> p c f", p=128), s_g)
    w_ready = s_g.now()
    p.emit("vector", lambda v: v.memset(ones1[:], 1.0), inc=s_v)
    for b in range(2):
        p.emit("vector", lambda v, b=b: v.memset(
            vtl[b][:].rearrange("p (h s) -> p h s", s=65)[:, :, 64:65], 1.0),
            inc=s_v)

    pend = None
    norm_done = {}
    for t in range(NBT):
        b = t % 2
        xwait = []
        if t >= 2:
            xwait = [(s_v, norm_done[t - 2]), (s_a, s_a.n)]
        p.dma("sync", xt[b][:], xb[128 * t:128 * (t + 1), :], s_ld[b],
              wait=xwait)
        ld = [s_ld[b].now()]
        if pend is not None:
            pend()
            pend = None
        m_ap, r_ap = _ln_stats(p, ld, xt[b][:], sq_scr[:], stats,
                               8 * (t % 2), s_v, s_a, LN_EPS, 1.0 / D)
        stats_a = s_a.now()

        def mk_norm(t=t, b=b, m_ap=m_ap, r_ap=r_ap, stats_a=stats_a):
            def go():
                p.emit("vector", lambda v: v.tensor_scalar(
                    lnx[b][:], xt[b][:], m_ap, r_ap,
                    ALU.subtract, ALU.mult), wait=[stats_a], inc=s_v)
                norm_done[t] = s_v.n
                bank = p_tr if t % 2 == 0 else p_trB
                tr_batch(p, ident, bank,
                         [lnx[b][:, 128 * c:128 * (c + 1)]
                          for c in range(DC)],
                         evacs=tr_views(bank, lnxT[:], DC, t, LBATCH),
                         evac_engines=("vector", "vector"),
                         extra_wait=[ident_ready],
                         first_wait=[(s_v, norm_done[t])])
            return go
        pend = mk_norm()
    pend()
    lnxT_done2 = tr_done(p)
    lnxT_done = lnxT_done2[0]

    for fc in range(DC):
        for th in range(2):
            for e in range(DC):
                for g in range(2):
                    p.emit("tensor", lambda te, fc=fc, th=th, e=e, g=g:
                           te.matmul(
                               psum_q[:, 512 * g:512 * (g + 1)],
                               wA[:, e * D + 128 * fc:e * D + 128 * (fc + 1)],
                               lnxT[:, e * LBATCH + 1024 * th + 512 * g:
                                    e * LBATCH + 1024 * th + 512 * (g + 1)],
                               start=(e == 0), stop=(e == DC - 1)),
                           wait=(lnxT_done2 + [w_ready, (s_a, s_a.n)]
                                 if (e == 0 and g == 0) else []),
                           inc=(s_mm if (e == DC - 1 and g == 1) else None))
            p.emit("scalar", lambda s, fc=fc, th=th: s.activation(
                qfm[:, fc * TOK + 1024 * th:fc * TOK + 1024 * (th + 1)],
                psum_q[:, 0:1024], AF.Relu, bias=bq_s[:, fc:fc + 1]),
                wait=[s_mm.now()], inc=s_a)
    q_done = s_a.now()
    q_mm_done = s_mm.now()
    p.dma("gpsimd", wA[:].rearrange("p (c f) -> p c f", c=DC),
          wvt.rearrange("(c p) f -> p c f", p=128), s_g,
          wait=[q_mm_done])
    wv_ready = s_g.now()
    ptrB_guard.__exit__(None, None, None)
    psum_v = st.enter_context(nc.psum_tensor("Apsum_v", [128, 1024], F32))

    kv_first = [True, True, True]
    kv_tile_done = {}
    pend_kv = None
    for t in range(NBT):
        b = t % 2
        kw = [(s_v, s_v.n), (s_a, s_a.n)]
        if t == 0:
            kw += [q_done, wv_ready]
        for half in range(2):
            for e in range(DC):
                p.emit("tensor", lambda te, t=t, e=e, half=half: te.matmul(
                    psum_q[:, 512 * half:512 * (half + 1)],
                    lnxT[:, e * LBATCH + 128 * t:e * LBATCH + 128 * (t + 1)],
                    wB[:, e * D + 512 * half:e * D + 512 * (half + 1)],
                    start=(e == 0), stop=False),
                    wait=(kw if (half == 0 and e == 0) else []))
            p.emit("tensor", lambda te, half=half: te.matmul(
                psum_q[:, 512 * half:512 * (half + 1)], ones1[:],
                bk_s[:, 512 * half:512 * (half + 1)], start=False,
                stop=True))
            for e in range(DC):
                p.emit("tensor", lambda te, t=t, e=e, half=half: te.matmul(
                    psum_v[:, 512 * half:512 * (half + 1)],
                    lnxT[:, e * LBATCH + 128 * t:e * LBATCH + 128 * (t + 1)],
                    wA[:, e * D + 512 * half:e * D + 512 * (half + 1)],
                    start=(e == 0), stop=False))
            p.emit("tensor", lambda te, half=half: te.matmul(
                psum_v[:, 512 * half:512 * (half + 1)], ones1[:],
                bv_s[:, 512 * half:512 * (half + 1)], start=False,
                stop=True), inc=(s_mm if half == 1 else None))
        proj_done = s_mm.now()
        if pend_kv is not None:
            pend_kv()
            pend_kv = None
        ewait = [proj_done]
        if t >= 2:
            ewait.append((s_mm, kv_tile_done[t - 2]))
        kview = psum_q[:].rearrange("p (h d) -> p h d", d=64)
        p.emit("scalar", lambda s, b=b, t=t, kview=kview: s.activation(
            ktl[b][:].rearrange("p (h d) -> p h d", d=128)[:, :, 0:64],
            kview, AF.Relu, scale=sin_s[:, t:t + 1]), wait=ewait)
        p.emit("scalar", lambda s, b=b, t=t, kview=kview: s.activation(
            ktl[b][:].rearrange("p (h d) -> p h d", d=128)[:, :, 64:128],
            kview, AF.Relu, scale=cos_s[:, t:t + 1]))
        k_evac = s_a.now()
        p.emit("vector", lambda v, b=b: v.tensor_copy(
            vtl[b][:].rearrange("p (h s) -> p h s", s=65)[:, :, 0:64],
            psum_v[:].rearrange("p (h d) -> p h d", d=64)), wait=ewait,
            inc=s_v)
        evac_done = s_v.now()

        def mk_kv(t=t, b=b, evac_done=evac_done, k_evac=k_evac):
            def go():
                for h in range(NH):
                    bank, off = KV_SLOT[h]
                    start = kv_first[bank]
                    kv_first[bank] = False
                    stop = (t == NBT - 1) and (KV_BANK_LAST[bank] == h)
                    p.emit("tensor", lambda te, h=h, off=off, start=start,
                           stop=stop, b=b: te.matmul(
                               psum_kv[:, off:off + 65],
                               ktl[b][:, 128 * h:128 * (h + 1)],
                               vtl[b][:, 65 * h:65 * (h + 1)],
                               start=start, stop=stop),
                           wait=([evac_done, k_evac] if h == 0 else []),
                           inc=(s_mm if h == NH - 1 else None))
                kv_tile_done[t] = s_mm.n
            return go
        pend_kv = mk_kv()
    pend_kv()
    pend_kv = None
    kv_mm_done = s_mm.now()
    p.dma("gpsimd", wB[:].rearrange("p (c f) -> p c f", c=DC),
          wot.rearrange("(c p) f -> p c f", p=128), s_g,
          wait=[kv_mm_done])
    wo_ready = s_g.now()

    for bank, (o_dst, width) in enumerate(((0, 325), (325, 325),
                                           (650, 390))):
        p.emit("vector", lambda v, bank=bank, o_dst=o_dst, width=width:
               v.tensor_copy(kvs[:, o_dst:o_dst + width],
                             psum_kv[:, 512 * bank:512 * bank + width]),
               wait=([kv_mm_done] if bank == 0 else []), inc=s_v)
    kv_evac = s_v.now()
    p.dma("sync", kvw[0:64, :], kvs[64:128, :], s_kv, wait=[kv_evac])
    p.dma("sync", kvw[64:128, :], kvs[0:64, :], s_kv)
    kv_ready = s_kv.now()

    def attn_slot(h, which):
        j = h // 2
        if h % 2 == 0:
            if j < 7:
                off = (0 if which == "top" else 512) + 65 * j
                return (psum_kv, off, ("kv", off // 512))
            return (psum_kv, 1024 + (0 if which == "top" else 65),
                    ("kv", 2))
        if j < 7:
            off = (0 if which == "top" else 512) + 65 * j
            return (psum_v, off, ("v", off // 512))
        return (psum_q, 512 + (0 if which == "top" else 65), ("q", 1))

    store_done = {}
    comb_psum_done = {}
    comb_a_done = {}
    wo_evac_done = {}
    pendA = None
    for t in range(NT):
        b = t % 2
        if t == 0:
            aw = [kv_ready, q_done, (s_v, s_v.n), (s_mm, s_mm.n)]
        else:
            aw = [(s_v, comb_psum_done[t - 1]), (s_a, comb_a_done[t - 1])]
        mm_list = []
        for h in range(NH):
            for which in ("top", "bot"):
                mm_list.append((h, which) + attn_slot(h, which))
        first_bank = {}
        last_bank = {}
        for i, (h, which, bp, boff, bank_id) in enumerate(mm_list):
            first_bank.setdefault(bank_id, i)
            last_bank[bank_id] = i
        for i, (h, which, bp, boff, bank_id) in enumerate(mm_list):
            par = (h % 2) * 64
            qsl = qfm[par:par + 64,
                      (h // 2) * TOK + 128 * t:(h // 2) * TOK + 128 * (t + 1)]
            if which == "top":
                src = kvs if h % 2 == 0 else kvw
            else:
                src = kvw if h % 2 == 0 else kvs
            p.emit("tensor", lambda te, qsl=qsl, src=src, par=par, h=h,
                   bp=bp, boff=boff,
                   start=(first_bank[bank_id] == i),
                   stop=(last_bank[bank_id] == i):
                   te.matmul(bp[:, boff:boff + 65], qsl,
                             src[par:par + 64, 65 * h:65 * (h + 1)],
                             start=start, stop=stop,
                             tile_position=(par, 0)),
                   wait=(aw if i == 0 else []),
                   inc=(s_mm if i == len(mm_list) - 1 else None))
        amm = s_mm.now()

        def a_view(b, par, j0, n):
            ap = acmb[b][:].rearrange("p (j s) -> p j s", s=130)
            return ap[:, j0:j0 + n, 65 * par:65 * par + 65]

        def ps_view(pt, off, n):
            if n > 1:
                return pt[:, off:off + 65 * n].rearrange(
                    "p (j s) -> p j s", s=65)
            return pt[:, off:off + 65]

        act_done = {}
        for par, pt, spill_p, spill_off in ((0, psum_kv, psum_kv, 1024),
                                            (1, psum_v, psum_q, 512)):
            p.emit("scalar", lambda s, b=b, par=par, pt=pt, t=t:
                   s.activation(acmb[b][:, 520 * par:520 * par + 455],
                                pt[:, 0:455],
                                AF.Copy, scale=sin_s[:, t:t + 1]),
                   wait=([amm, (s_v, s_v.n)] if par == 0 else []))
            p.emit("scalar", lambda s, b=b, par=par, sp=spill_p,
                   so=spill_off, t=t: s.activation(
                       acmb[b][:, 520 * par + 455:520 * par + 520],
                       sp[:, so:so + 65],
                       AF.Copy, scale=sin_s[:, t:t + 1]))
            act_done[par] = s_a.n
        for par, pt, spill_p, spill_off in ((0, psum_kv, psum_kv, 1024),
                                            (1, psum_v, psum_q, 512)):
            p.emit("vector", lambda v, b=b, par=par, pt=pt, t=t:
                   v.scalar_tensor_tensor(
                       acmb[b][:, 520 * par:520 * par + 455],
                       pt[:, 512:512 + 455],
                       cos_s[:, t:t + 1],
                       acmb[b][:, 520 * par:520 * par + 455],
                       ALU.mult, ALU.add),
                   wait=[(s_a, act_done[par])] + ([amm] if par == 0 else []),
                   inc=s_v)
            p.emit("vector", lambda v, b=b, par=par, sp=spill_p,
                   so=spill_off, t=t: v.scalar_tensor_tensor(
                       acmb[b][:, 520 * par + 455:520 * par + 520],
                       sp[:, so + 65:so + 130],
                       cos_s[:, t:t + 1],
                       acmb[b][:, 520 * par + 455:520 * par + 520],
                       ALU.mult, ALU.add), inc=s_v)
        comb_psum_done[t] = s_v.n
        comb_a_done[t] = s_a.n
        p.emit("vector", lambda v, b=b: v.tensor_scalar(
            dens[b][:],
            acmb[b][:].rearrange("p (h s) -> p h s", s=65)[:, :, 64:65],
            COS_EPS, None, ALU.max), inc=s_v)
        p.emit("vector", lambda v, b=b: v.reciprocal(zrec[b][:], dens[b][:]),
               inc=s_v)
        zr = s_v.now()
        for par in range(2):
            p.emit("vector", lambda v, b=b, par=par: v.tensor_tensor(
                abf[b][:].rearrange(
                    "p (j r) -> p j r", r=128)[:, :, 64 * par:64 * par + 64],
                acmb[b][:, 520 * par:520 * par + 520].rearrange(
                    "p (j s) -> p j s", s=65)[:, :, 0:64],
                zrec[b][:, 8 * par:8 * par + 8].unsqueeze(2).broadcast_to(
                    (128, 8, 64)),
                ALU.mult), inc=s_v)
        attn_done = s_v.now()
        attn_done_a = s_a.now()
        if pendA is not None:
            pendA()
            pendA = None

        def mk_tailA(t=t, b=b, attn_done=attn_done, attn_done_a=attn_done_a):
            def go():
                tr_batch(p, ident, p_tr,
                         [abf[b][:, 128 * c:128 * (c + 1)]
                          for c in range(DC)],
                         atT[b][:],
                         first_wait=[attn_done, attn_done_a])
                att_T2 = tr_done(p)
                for half in range(2):
                    if half == 0:
                        wo_w = att_T2 + [wo_ready]
                        if t >= 2:
                            wo_w.append((s_a, wo_evac_done[t - 2]))
                        elif t == 0:
                            wo_w.append((s_a, s_a.n))
                    else:
                        wo_w = [(s_a, s_a.n)]
                    for c in range(DC):
                        p.emit("tensor", lambda te, b=b, c=c, half=half:
                               te.matmul(psum_q[:, 0:512],
                                         atT[b][:, 128 * c:128 * (c + 1)],
                                         wB[:, c * D + 512 * half:
                                            c * D + 512 * (half + 1)],
                                         start=(c == 0), stop=False),
                               wait=(wo_w if c == 0 else []))
                    p.emit("tensor", lambda te, half=half: te.matmul(
                        psum_q[:, 0:512], ones1[:],
                        bo_s[:, 512 * half:512 * (half + 1)],
                        start=False, stop=True), inc=s_mm)
                    hw = [s_mm.now()]
                    if t >= 2 and half == 0:
                        hw.append((s_st[t % 2], store_done[t - 2]))
                    p.emit("scalar", lambda s, b=b, half=half: s.activation(
                        hsb[b][:, 512 * half:512 * (half + 1)],
                        psum_q[:, 0:512], AF.Copy),
                        wait=hw)
                wo_evac_done[t] = s_a.n
                h_done = s_a.now()
                p.dma("sync", xt[b][:], xb[128 * t:128 * (t + 1), :], s_x,
                      wait=[h_done])
                m_ap, r_ap = _ln_stats(p, [h_done], hsb[b][:], sq_scr[:],
                                       stats, 8 * (t % 2), s_v, s_a, LN_EPS,
                                       1.0 / D)
                p.emit("vector", lambda v, b=b, m_ap=m_ap, r_ap=r_ap:
                       v.tensor_scalar(hsb[b][:], hsb[b][:], m_ap, r_ap,
                                       ALU.subtract, ALU.mult),
                       wait=[s_a.now()], inc=s_v)
                p.emit("vector", lambda v, b=b: v.tensor_tensor(
                    hsb[b][:], hsb[b][:], xt[b][:], ALU.add),
                    wait=[(s_x, s_x.n)], inc=s_v)
                p.dma("sync", x1o[128 * t:128 * (t + 1), :], hsb[b][:],
                      s_st[b], wait=[s_v.now()])
                store_done[t] = s_st[b].n
            return go
        pendA = mk_tailA()
    pendA()
    p.emit("sync", lambda sy: sy.nop(),
           wait=[s_st[0].now(), s_st[1].now()])



def build_A():
    nc = bass.Bass("TRN2", target_bir_lowering=False, debug=False)
    io = declare_io_A(nc)
    st = contextlib.ExitStack()
    with st:
        p = Prog(nc, st)
        make_sems(p)
        emit_A(nc, st, p, io)
        p.finalize()
    return nc


def _prep_A(inputs, core):
    b = core // 2
    half = core % 2
    x = np.asarray(inputs["L0_attn_x"]).reshape(4, LBATCH, D)
    xb = x[b]
    sin, cos = _sincos_tables()
    if half == 1:
        xb = np.concatenate([xb[TOK:], xb[:TOK]], axis=0)
        sin = np.concatenate([sin[:, NT:], sin[:, :NT]], axis=1)
        cos = np.concatenate([cos[:, NT:], cos[:, :NT]], axis=1)
    return {
        "xb": np.ascontiguousarray(xb),
        "wqt": np.ascontiguousarray(np.asarray(inputs["L0_attn_Wq"]).T),
        "wkt": np.ascontiguousarray(np.asarray(inputs["L0_attn_Wk"]).T),
        "wvt": np.ascontiguousarray(np.asarray(inputs["L0_attn_Wv"]).T),
        "wot": np.ascontiguousarray(np.asarray(inputs["L0_attn_Wo"]).T),
        "bqpp": np.ascontiguousarray(
            np.asarray(inputs["L0_attn_bq"]).reshape(DC, 128).T),
        "bkr": np.asarray(inputs["L0_attn_bk"]).reshape(1, D),
        "bvr": np.asarray(inputs["L0_attn_bv"]).reshape(1, D),
        "bor": np.asarray(inputs["L0_attn_bo"]).reshape(1, D),
        "sinp": np.ascontiguousarray(sin),
        "cosp": np.ascontiguousarray(cos),
    }


def run_A(inputs):
    nc = build_A()
    in_maps = [{"A" + k: v for k, v in _prep_A(inputs, c).items()} for c in range(NCORES)]
    res = run_bass_kernel_spmd(nc, in_maps, core_ids=list(range(NCORES)))
    return np.concatenate([r["x1"] for r in res.results], axis=0)





SEL_BF = np.zeros((4, 8, 128), dtype=BF)
for _c in range(4):
    SEL_BF[_c, 2 * _c, 0:64] = 1
    SEL_BF[_c, 2 * _c + 1, 64:128] = 1

SEL64_BF = np.zeros((8, 8, 64), dtype=BF)
for _e in range(8):
    SEL64_BF[_e, _e, :] = 1


def declare_io_B(nc, scratch=None):
    io = {}
    io["x1i"] = (scratch["x1i"] if scratch and "x1i" in scratch else nc.dram_tensor("Bx1i", [TOK, D], F32, kind="ExternalInput").ap())
    io["w1at"] = nc.dram_tensor("Bw1at", [D, 512], F32, kind="ExternalInput").ap()
    io["w1b"] = nc.dram_tensor("Bw1b", [512, 4096], F32, kind="ExternalInput").ap()
    io["w2at"] = nc.dram_tensor("Bw2at", [4096, 512], F32, kind="ExternalInput").ap()
    io["w2b"] = nc.dram_tensor("Bw2b", [512, D], F32, kind="ExternalInput").ap()
    io["eww"] = nc.dram_tensor("Beww", [TOK, 2], F32, kind="ExternalInput").ap()
    io["eii"] = nc.dram_tensor("Beii", [TOK, 2], I32, kind="ExternalInput").ap()
    io["x2o"] = (scratch["x2o"] if scratch and "x2o" in scratch else nc.dram_tensor("Bx2", [TOK, D], F32, kind="ExternalOutput").ap())
    return io


def emit_B(nc, st, p, io):
    x1i = io["x1i"]
    w1at = io["w1at"]
    w1b = io["w1b"]
    w2at = io["w2at"]
    w2b = io["w2b"]
    eww = io["eww"]
    eii = io["eii"]
    x2o = io["x2o"]
    sb = lambda name, shape, dt=BF16: st.enter_context(
        nc.sbuf_tensor("BB" + name, shape, dt))
    w1at_s = sb("w1at_s", [128, DC * 512])
    w1b_s = sb("w1b_s", [128, 4 * 4096])
    w2at_s = sb("w2at_s", [128, 32 * 512])
    w2b_s = sb("w2b_s", [64, 8 * D])
    t1_sb = sb("t1_sb", [128, 4 * TOK])
    t3_sb = sb("t3_sb", [64, 8 * TOK])
    ident = sb("ident", [128, 128])
    sel64 = sb("sel64", [8, 8 * 64])
    eif = sb("eif", [128, NT * 2])
    ewf = sb("ewf", [128, NT * 2])
    c_all = sb("c_all", [128, NT * 8], F32)
    c_allb = sb("c_allb", [128, NT * 8])
    c_fm = sb("c_fm", [8, TOK])
    ceq = sb("ceq", [128, NT * 2])
    cpr = sb("cpr", [128, NT * 2])
    xt = [sb(f"xt{i}", [128, D], F32) for i in range(2)]
    lnx = [sb(f"lnx{i}", [128, D]) for i in range(2)]
    ysb = [sb(f"ysb{i}", [128, D], F32) for i in range(2)]
    sq_scr = sb("sq_scr", [128, D])
    stats = sb("stats", [128, 16], F32)
    lnxT_guard = nc.sbuf_tensor("BlnxT", [128, DC * TOK], BF16)
    lnxT = lnxT_guard.__enter__()
    ps = lambda name, shape, dt=F32: st.enter_context(
        nc.psum_tensor("BB" + name, shape, dt))
    psum2e = ps("psum2e", [128, 1024])
    psum2o = ps("psum2o", [128, 1024])
    ptr_guard = nc.psum_tensor("Bp_tr", [128, 1024], BF16)
    p_tr = ptr_guard.__enter__()
    ptrB_guard = nc.psum_tensor("Bp_trB", [128, 1024], BF16)
    p_trB = ptrB_guard.__enter__()

    S = p.S
    s_ld = S.s_ld
    s_w = S.s_w
    s_g = S.s_g
    s_mm = S.s_mm
    s_v = S.s_v
    s_a = S.s_a
    s_st = S.s_st
    s_x = S.s_x

    p.dma("sync", ident[:], nc.inline_tensor(IDENT_BF, "identB").ap(), s_w)
    p.dma("sync", sel64[:].rearrange("p (e m) -> p e m", e=8),
          nc.inline_tensor(
              np.ascontiguousarray(SEL64_BF.transpose(1, 0, 2)),
              "sel64B").ap(), s_w)
    consts_ready = s_w.now()
    p.dma("gpsimd", w1at_s[:].rearrange("p (c r) -> p c r", c=DC),
          w1at.rearrange("(c p) r -> p c r", p=128), s_g)
    p.dma("gpsimd", w1b_s[:].rearrange("p (c f) -> p c f", c=4),
          w1b.rearrange("(c p) f -> p c f", p=128), s_g)
    p.dma("gpsimd", w2b_s[:].rearrange("p (e f) -> p e f", e=8),
          w2b.rearrange("(e p) f -> p e f", p=64), s_g)
    for blk in range(4):
        p.dma("gpsimd",
              w2at_s[:, blk * 8 * 512:(blk + 1) * 8 * 512].rearrange(
                  "p (c r) -> p c r", c=8),
              w2at[blk * 1024:(blk + 1) * 1024, :].rearrange(
                  "(c p) r -> p c r", p=128), s_g)
    p.dma("gpsimd", eif[:].rearrange("p (t two) -> p t two", two=2),
          eii.rearrange("(t p) two -> p t two", p=128), s_g)
    p.dma("gpsimd", ewf[:].rearrange("p (t two) -> p t two", two=2),
          eww.rearrange("(t p) two -> p t two", p=128), s_g)
    w_ready = s_g.now()

    cw = [w_ready]
    for e in range(8):
        p.emit("vector", lambda v, e=e: v.tensor_scalar(
            ceq[:], eif[:], float(e), None, ALU.is_equal), wait=cw)
        cw = []
        p.emit("vector", lambda v: v.tensor_tensor(
            cpr[:], ceq[:], ewf[:], ALU.mult))
        p.emit("vector", lambda v, e=e: v.reduce_sum(
            c_all[:].rearrange("p (t e) -> p t e", e=8)[:, :, e:e + 1],
            cpr[:].rearrange("p (t d) -> p t d", d=2), axis=AX.X))
    p.emit("vector", lambda v: v.tensor_copy(c_allb[:], c_all[:]))

    for t in range(NT):
        b = t % 2
        xwait = [(s_v, s_v.n), (s_a, s_a.n)] if t >= 2 else []
        p.dma("sync", xt[b][:], x1i[128 * t:128 * (t + 1), :], s_ld[b],
              wait=xwait)
        ld = [s_ld[b].now()]
        m_ap, r_ap = _ln_stats(p, ld, xt[b][:], sq_scr[:], stats,
                               8 * (t % 2), s_v, s_a, LN_EPS, 1.0 / D)
        p.emit("vector", lambda v, b=b, m_ap=m_ap, r_ap=r_ap:
               v.tensor_scalar(lnx[b][:], xt[b][:], m_ap, r_ap,
                               ALU.subtract, ALU.mult),
               wait=[s_a.now()])
        tr_batch(p, ident, p_trB,
                 [lnx[b][:, 128 * c:128 * (c + 1)] for c in range(DC)],
                 evacs=tr_views(p_trB, lnxT[:], DC, t, TOK),
                 evac_engines=("vector", "vector"),
                 extra_wait=[consts_ready])
        p.emit("tensor", lambda te, t=t: te.transpose(
            p_tr[0:8, 512 + 128 * (t % 4):512 + 128 * (t % 4) + 128],
            c_allb[:, 8 * t:8 * (t + 1)], ident[:]),
            wait=[(s_v, s_v.n)], inc=s_mm)
        p.emit("vector", lambda v, t=t: v.tensor_copy(
            c_fm[0:8, 128 * t:128 * (t + 1)],
            p_tr[0:8, 512 + 128 * (t % 4):512 + 128 * (t % 4) + 128]),
            wait=[s_mm.now()])
    lnxT_done = s_v.now()
    ptr_mm_done = s_mm.now()

    crep_done = s_v.now()

    for rc in range(4):
        for tg in range(4):
            dst = psum2e if tg < 2 else psum2o
            col = 512 * (tg % 2)
            for e in range(DC):
                p.emit("tensor", lambda te, rc=rc, tg=tg, e=e, dst=dst,
                       col=col: te.matmul(
                           dst[:, col:col + 512],
                           w1at_s[:, e * 512 + 128 * rc:
                                  e * 512 + 128 * (rc + 1)],
                           lnxT[:, e * TOK + 512 * tg:
                                e * TOK + 512 * (tg + 1)],
                           start=(e == 0), stop=(e == DC - 1)),
                       wait=([crep_done, w_ready, (s_v, s_v.n),
                              (p.S.s_a, p.S.s_a.n)]
                             if (tg == 0 and e == 0) else []),
                       inc=(s_mm if e == DC - 1 else None))
        ev1 = [s_mm.now()]
        p.emit("vector", lambda v, rc=rc: v.tensor_copy(
            t1_sb[:, rc * TOK:rc * TOK + 1024], psum2e[:, 0:1024]),
            wait=ev1)
        p.emit("vector", lambda v, rc=rc: v.tensor_copy(
            t1_sb[:, rc * TOK + 1024:rc * TOK + 2048], psum2o[:, 0:1024]))
    t1_done = s_v.now()

    for e in range(8):
        for tg in range(4):
            p.emit("tensor", lambda te, e=e, tg=tg: te.matmul(
                psum2e[0:64, 512 * (tg % 2):512 * (tg % 2) + 512],
                sel64[:, 64 * e:64 * (e + 1)],
                c_fm[0:8, 512 * tg:512 * (tg + 1)],
                start=True, stop=True),
                wait=(([t1_done, consts_ready] if tg == 0 else []) +
                      ([(s_v, s_v.n)] if tg % 2 == 0 else [])),
                inc=(s_mm if tg % 2 == 1 else None))
            if tg % 2 == 1:
                p.emit("vector", lambda v, e=e, tg=tg: v.tensor_copy(
                    crep[0:64, e * TOK + 512 * (tg - 1):
                         e * TOK + 512 * (tg + 1)],
                    psum2e[0:64, 0:1024]), wait=[s_mm.now()])
    crep_ready = s_v.now()

    lnxT_guard.__exit__(None, None, None)
    hbuf = [st.enter_context(nc.sbuf_tensor(f"Bhbuf{i}", [128, 2048], BF16))
            for i in range(2)]
    crep = st.enter_context(nc.sbuf_tensor("Bcrep", [64, 8 * TOK], BF16))
    ptrB_guard.__exit__(None, None, None)
    ptr_guard.__exit__(None, None, None)
    p3e = st.enter_context(nc.psum_tensor("p3e", [64, 1024], F32))
    p3o = st.enter_context(nc.psum_tensor("p3o", [64, 1024], F32))

    gelu_done = {}
    gelu_done_par = {}
    t2mm_par = {}
    t3_done = {}
    pending_t3 = None
    it = 0
    for th in range(2):
        for c in range(4):
            p3_first = {("e", 0): True, ("e", 1): True,
                        ("o", 0): True, ("o", 1): True}
            pending_t3 = None
            for fc in range(32):
                for par, dst in ((0, psum2e), (64, psum2o)):
                    t2w = [t1_done, ptr_mm_done, crep_ready] if it == 0 \
                        else []
                    if it > 0:
                        t2w = [(s_a, gelu_done_par[(it - 1, par)])]
                    for g in range(2):
                        last = (par == 64 and g == 1)
                        p.emit("tensor", lambda te, c=c, fc=fc, th=th,
                               par=par, dst=dst, g=g: te.matmul(
                                   dst[:, 512 * g:512 * (g + 1)],
                                   w1b_s[par:par + 64,
                                         c * 4096 + 128 * fc:
                                         c * 4096 + 128 * (fc + 1)],
                                   t1_sb[par:par + 64,
                                         c * TOK + 1024 * th + 512 * g:
                                         c * TOK + 1024 * th + 512 * (g + 1)],
                                   start=True, stop=True,
                                   tile_position=(par, 0)),
                               wait=(t2w if g == 0 else []),
                               inc=(s_mm if (g == 1) else None))
                    t2mm_par[(it, par)] = s_mm.n
                t2mm = s_mm.now()
                if pending_t3 is not None:
                    pending_t3()
                    pending_t3 = None
                hb = hbuf[it % 2]
                gw = [(s_mm, t2mm_par[(it, 0)])]
                if it >= 2 and (it - 2) in t3_done:
                    gw.append((s_mm, t3_done[it - 2]))
                p.emit("scalar", lambda s, hb=hb: s.activation(
                    hb[:, 0:1024], psum2e[:, 0:1024], AF.Gelu), wait=gw)
                gelu_done_par[(it, 0)] = s_a.n
                p.emit("scalar", lambda s, hb=hb: s.activation(
                    hb[:, 1024:2048], psum2o[:, 0:1024], AF.Gelu),
                    wait=[(s_mm, t2mm_par[(it, 64)])])
                gelu_done_par[(it, 64)] = s_a.n
                gelu_done[it] = s_a.n

                def make_t3(c=c, fc=fc, it=it, hb=hb, pf=p3_first):
                    def go():
                        t3w = [(s_a, gelu_done[it])]
                        if fc == 0:
                            t3w.append((s_v, s_v.n))
                        for pk, dst3 in (("e", p3e), ("o", p3o)):
                            for g in range(2):
                                start = pf[(pk, g)]
                                pf[(pk, g)] = False
                                stop = (fc == 31)
                                last = (fc == 31 and pk == "o" and g == 1)
                                oddoff = 64 if pk == "o" else 0
                                p.emit("tensor", lambda te, c=c, fc=fc, g=g,
                                       dst3=dst3, oddoff=oddoff, hb=hb,
                                       start=start, stop=stop: te.matmul(
                                           dst3[0:64, 512 * g:512 * (g + 1)],
                                           w2at_s[:, fc * 512 + 128 * c +
                                                  oddoff:
                                                  fc * 512 + 128 * c +
                                                  oddoff + 64],
                                           hb[:, 1024 * (oddoff // 64) +
                                              512 * g:
                                              1024 * (oddoff // 64) +
                                              512 * (g + 1)],
                                           start=start, stop=stop),
                                       wait=(t3w if (pk == "e" and g == 0)
                                             else []),
                                       inc=(s_mm if last else None))
                        t3_done[it] = s_mm.n
                    return go
                pending_t3 = make_t3()
                it += 1
            pending_t3()
            pending_t3 = None
            p.emit("vector", lambda v, c=c, th=th: v.tensor_copy(
                t3_sb[0:64, (2 * c) * TOK + 1024 * th:
                      (2 * c) * TOK + 1024 * (th + 1)],
                p3e[0:64, 0:1024]), wait=[s_mm.now()])
            p.emit("vector", lambda v, c=c, th=th: v.tensor_copy(
                t3_sb[0:64, (2 * c + 1) * TOK + 1024 * th:
                      (2 * c + 1) * TOK + 1024 * (th + 1)],
                p3o[0:64, 0:1024]))
    t3_all = s_v.now()

    for e in range(8):
        p.emit("vector", lambda v, e=e: v.tensor_tensor(
            t3_sb[0:64, e * TOK:(e + 1) * TOK],
            t3_sb[0:64, e * TOK:(e + 1) * TOK],
            crep[0:64, e * TOK:(e + 1) * TOK], ALU.mult),
            wait=([t3_all, crep_ready] if e == 0 else []))
    t3s_done = s_v.now()

    store_done = {}
    for t in range(NT):
        b = t % 2
        for half in range(2):
            dst = psum2e if half == 0 else psum2o
            for e in range(8):
                w4 = [t3s_done, (s_v, s_v.n)] if e == 0 else []
                p.emit("tensor", lambda te, t=t, e=e, half=half, dst=dst:
                       te.matmul(
                           dst[:, 0:512],
                           t3_sb[0:64, e * TOK + 128 * t:
                                 e * TOK + 128 * (t + 1)],
                           w2b_s[0:64, e * D + 512 * half:
                                 e * D + 512 * (half + 1)],
                           start=(e == 0), stop=(e == 7)),
                       wait=w4, inc=(s_mm if e == 7 else None))
            hw0 = [s_mm.now()]
            if t >= 2:
                hw0.append((s_st[t % 2], store_done[t - 2]))
            p.emit("vector", lambda v, b=b, half=half, dst=dst:
                   v.tensor_copy(ysb[b][:, 512 * half:512 * (half + 1)],
                                 dst[:, 0:512]), wait=hw0)
        y_done = s_v.now()
        p.dma("sync", xt[b][:], x1i[128 * t:128 * (t + 1), :], s_x,
              wait=[y_done])
        m_ap, r_ap = _ln_stats(p, [y_done], ysb[b][:], sq_scr[:], stats,
                               8 * (t % 2), s_v, s_a, LN_EPS, 1.0 / D)
        p.emit("vector", lambda v, b=b, m_ap=m_ap, r_ap=r_ap:
               v.tensor_scalar(ysb[b][:], ysb[b][:], m_ap, r_ap,
                               ALU.subtract, ALU.mult),
               wait=[s_a.now()])
        p.emit("vector", lambda v, b=b: v.tensor_tensor(
            ysb[b][:], ysb[b][:], xt[b][:], ALU.add),
            wait=[(s_x, s_x.n)])
        p.dma("sync", x2o[128 * t:128 * (t + 1), :], ysb[b][:], s_st[b],
              wait=[s_v.now()])
        store_done[t] = s_st[b].n
    p.emit("sync", lambda sy: sy.nop(),
           wait=[s_st[0].now(), s_st[1].now()])


def build_B():
    nc = bass.Bass("TRN2", target_bir_lowering=False, debug=False)
    io = declare_io_B(nc)
    st = contextlib.ExitStack()
    with st:
        p = Prog(nc, st)
        make_sems(p)
        emit_B(nc, st, p, io)
        p.finalize()
    return nc


def _prep_B(inputs, core, x1=None):
    t0 = core * TOK
    ew = np.asarray(inputs["L0_ffn_expert_weights"])
    ei = np.asarray(inputs["L0_ffn_expert_indices"])
    n_active = ew.shape[0]
    if t0 + TOK <= n_active:
        ew_c, ei_c = ew[t0:t0 + TOK], ei[t0:t0 + TOK]
    else:
        ew_c = np.zeros((TOK, 2), np.float32)
        ei_c = np.zeros((TOK, 2), np.int32)
        if t0 < n_active:
            ew_c[:n_active - t0] = ew[t0:n_active]
            ei_c[:n_active - t0] = ei[t0:n_active]
    d = {
        "w1at": np.ascontiguousarray(np.asarray(inputs["L0_ffn_w1_A"]).T),
        "w1b": np.ascontiguousarray(np.asarray(inputs["L0_ffn_w1_B"])),
        "w2at": np.ascontiguousarray(np.asarray(inputs["L0_ffn_w2_A"]).T),
        "w2b": np.ascontiguousarray(np.asarray(inputs["L0_ffn_w2_B"])),
        "eww": np.ascontiguousarray(ew_c),
        "eii": np.ascontiguousarray(ei_c),
    }
    if x1 is not None:
        d["x1i"] = np.ascontiguousarray(x1[t0:t0 + TOK])
    return d


def run_B(inputs, x1):
    nc = build_B()
    in_maps = [{"B" + k: v for k, v in _prep_B(inputs, c, x1).items()} for c in range(NCORES)]
    res = run_bass_kernel_spmd(nc, in_maps, core_ids=list(range(NCORES)))
    return np.concatenate([r["x2"] for r in res.results], axis=0)



def declare_io_C(nc, scratch=None):
    io = {}
    io["x2i"] = (scratch["x2i"] if scratch and "x2i" in scratch else nc.dram_tensor("Cx2i", [TOK, D], F32, kind="ExternalInput").ap())
    io["srcT"] = nc.dram_tensor("CsrcT", [D, TOK], F32, kind="ExternalInput").ap()
    io["valT"] = nc.dram_tensor("CvalT", [D, TOK], F32, kind="ExternalInput").ap()
    io["wq"] = nc.dram_tensor("Cwq", [D, D], F32, kind="ExternalInput").ap()
    io["wk"] = nc.dram_tensor("Cwk", [D, D], F32, kind="ExternalInput").ap()
    io["wv"] = nc.dram_tensor("Cwv", [D, D], F32, kind="ExternalInput").ap()
    io["wo"] = nc.dram_tensor("Cwo", [D, D], F32, kind="ExternalInput").ap()
    io["bqr"] = nc.dram_tensor("Cbqr", [1, D], F32, kind="ExternalInput").ap()
    io["bkr"] = nc.dram_tensor("Cbkr", [1, D], F32, kind="ExternalInput").ap()
    io["bvr"] = nc.dram_tensor("Cbvr", [1, D], F32, kind="ExternalInput").ap()
    io["bor"] = nc.dram_tensor("Cbor", [1, D], F32, kind="ExternalInput").ap()
    io["lngr"] = nc.dram_tensor("Clngr", [1, D], F32, kind="ExternalInput").ap()
    io["lnbr"] = nc.dram_tensor("Clnbr", [1, D], F32, kind="ExternalInput").ap()
    io["x3o"] = (scratch["x3o"] if scratch and "x3o" in scratch else nc.dram_tensor("Cx3", [TOK, D], F32, kind="ExternalOutput").ap())
    return io


def emit_C(nc, st, p, io):
    x2i = io["x2i"]
    srcT = io["srcT"]
    valT = io["valT"]
    wq = io["wq"]
    wk = io["wk"]
    wv = io["wv"]
    wo = io["wo"]
    bqr = io["bqr"]
    bkr = io["bkr"]
    bvr = io["bvr"]
    bor = io["bor"]
    lngr = io["lngr"]
    lnbr = io["lnbr"]
    x3o = io["x3o"]
    sb = lambda name, shape, dt=BF16: st.enter_context(
        nc.sbuf_tensor("CC" + name, shape, dt))
    wbA = sb("wbA", [128, DC * D])
    wbB = sb("wbB", [128, DC * D])
    ident = sb("ident", [128, 128])
    onesT = sb("onesT", [128, 128])
    b_rows = sb("b_rows", [1, 4 * D])
    g_rows = sb("g_rows", [1, 2 * D])
    grepl = sb("grepl", [128, D])
    brepl = sb("brepl", [128, D])
    z_sb = sb("z_sb", [128, 1024])
    zline = sb("zline", [128, 1024])
    tmp_o = sb("tmp_o", [64, 1024])
    xt = [sb("xt0", [128, D], F32)] * 2
    lnx = [sb("lnx0", [128, D])] * 2
    ysb = [sb("ysb0", [128, D], F32)] * 2
    tmpf = sb("tmpf", [128, D], F32)
    sqb = sb("sqb", [128, D])
    qn_t = lnx[0]
    dzf = tmpf
    stats = st.enter_context(
        nc.sbuf_tensor("Cstats", [128, 48], F32))
    bigT_g = nc.sbuf_tensor("CbigT", [128, DC * TOK], BF16)
    bigT = bigT_g.__enter__()
    qnT_g = nc.sbuf_tensor("CqnT", [128, DC * TOK], BF16)
    qnT = qnT_g.__enter__()
    knT_g = nc.sbuf_tensor("CknT", [128, DC * TOK], BF16)
    knT = knT_g.__enter__()
    vnf_g = nc.sbuf_tensor("Cvn_f", [128, NT * NH * 65], BF16)
    vn_f = vnf_g.__enter__()
    ab_g = nc.sbuf_tensor("CAbuf", [128, 2048], BF16)
    u_g = nc.sbuf_tensor("Cu2048", [128, 2048], BF16)
    Abuf = ab_g.__enter__()
    u2048 = u_g.__enter__()
    Abuf2_pair = [Abuf, u2048]
    tminf = u2048[:, 0:1024]
    eqb = u2048[:, 1024:2048]
    sq_scr = Abuf[:, 0:1024]
    ps = lambda name, shape, dt=F32: st.enter_context(
        nc.psum_tensor("CC" + name, shape, dt))
    sce_guard = nc.psum_tensor("Csc_e", [128, 1024], F32)
    sc_e = sce_guard.__enter__()
    sco_guard = nc.psum_tensor("Csc_o", [128, 1024], F32)
    sc_o = sco_guard.__enter__()
    ptr_guard = nc.psum_tensor("Cp_tr", [128, 1024], BF16)
    p_tr = ptr_guard.__enter__()
    ptrB_guard = nc.psum_tensor("Cp_trB", [128, 1024], BF16)
    p_trB = ptrB_guard.__enter__()

    S = p.S
    s_ld = S.s_ld
    s_w = S.s_w
    s_g = S.s_g
    s_mm = S.s_mm
    s_v = S.s_v
    s_a = S.s_a
    s_st = S.s_st
    s_x = S.s_x
    s_ctx = S.s_ctx

    p.dma("sync", ident[:], nc.inline_tensor(IDENT_BF, "identC").ap(), s_w)
    consts_ready = s_w.now()
    p.dma("gpsimd", b_rows[:, 0 * D:1 * D], bqr, s_g)
    p.dma("gpsimd", b_rows[:, 1 * D:2 * D], bkr, s_g)
    p.dma("gpsimd", b_rows[:, 2 * D:3 * D], bvr, s_g)
    p.dma("gpsimd", b_rows[:, 3 * D:4 * D], bor, s_g)
    p.dma("gpsimd", g_rows[:, 0 * D:1 * D], lngr, s_g)
    p.dma("gpsimd", g_rows[:, 1 * D:2 * D], lnbr, s_g)
    p.dma("gpsimd", wbA[:].rearrange("p (c f) -> p c f", c=DC),
          wq.rearrange("(c p) f -> p c f", p=128), s_g)
    p.dma("gpsimd", wbB[:].rearrange("p (c f) -> p c f", c=DC),
          wk.rearrange("(c p) f -> p c f", p=128), s_g)
    w_ready = s_g.now()
    p.emit("vector", lambda v: v.memset(onesT[:], 1.0))
    for t in range(NT):
        p.emit("vector", lambda v, t=t: v.memset(
            vn_f[:, t * NH * 65:(t + 1) * NH * 65].rearrange(
                "p (h s) -> p h s", s=65)[:, :, 64:65], 1.0))
    ones_done = s_v.now()

    for which, dst in ((0, grepl), (1, brepl)):
        for g in range(2):
            p.emit("tensor", lambda te, which=which, g=g: te.matmul(
                sc_e[:, 512 * g:512 * (g + 1)], onesT[0:1, :],
                g_rows[:, which * D + 512 * g:which * D + 512 * (g + 1)],
                start=True, stop=True),
                wait=([ones_done, w_ready, (s_v, s_v.n)] if g == 0 else []),
                inc=(s_mm if g == 1 else None))
        p.emit("vector", lambda v, dst=dst: v.tensor_copy(
            dst[:], sc_e[:, 0:1024]), wait=[s_mm.now()])
    gb_done = s_v.now()

    def elu_norm_tile(psum_src, dst_tile, stats_g, w0=()):
        p.emit("vector", lambda v: v.tensor_scalar(
            tminf, psum_src[:, 0:1024], 0.0, None, ALU.min), wait=w0)
        p.emit("scalar", lambda s: s.activation(
            eqb, tminf, AF.Exp), wait=[s_v.now()])
        p.emit("vector", lambda v: v.scalar_tensor_tensor(
            eqb, eqb, -1.0, psum_src[:, 0:1024],
            ALU.add, ALU.max), wait=[s_a.now()])
        p.emit("vector", lambda v: v.tensor_tensor(
            sqb[:], eqb, eqb, ALU.mult))
        p.emit("vector", lambda v, stats_g=stats_g: v.reduce_sum(
            stats[:, stats_g:stats_g + 16],
            sqb[:].rearrange("p (h d) -> p h d", d=64), axis=AX.X))
        p.emit("vector", lambda v, stats_g=stats_g: v.tensor_scalar(
            stats[:, stats_g:stats_g + 16], stats[:, stats_g:stats_g + 16],
            1e-30, None, ALU.add))
        p.emit("scalar", lambda s, stats_g=stats_g: s.activation(
            stats[:, stats_g + 16:stats_g + 32],
            stats[:, stats_g:stats_g + 16], AF.Ln), wait=[s_v.now()])
        p.emit("scalar", lambda s, stats_g=stats_g: s.activation(
            stats[:, stats_g:stats_g + 16],
            stats[:, stats_g + 16:stats_g + 32], AF.Exp, scale=-0.5))
        qn_a = s_a.now()
        p.emit("vector", lambda v, stats_g=stats_g, dst_tile=dst_tile:
               v.tensor_tensor(
                   dst_tile[:, 0:1024].rearrange("p (h d) -> p h d", d=64),
                   u2048[:, 1024:2048].rearrange("p (h d) -> p h d", d=64),
                   stats[:, stats_g:stats_g + 16].unsqueeze(2).broadcast_to(
                       (128, 16, 64)),
                   ALU.mult),
               wait=[qn_a, (s_mm, s_mm.n)], inc=s_v)
        return [(s_v, s_v.n), (s_a, s_a.n)]

    for t in range(NT):
        b = t % 2
        xwait = [(s_v, s_v.n), (s_a, s_a.n)] if t >= 1 else []
        p.dma("sync", xt[b][:], x2i[128 * t:128 * (t + 1), :], s_ld[b],
              wait=xwait)
        ld = [s_ld[b].now()]
        m_ap, r_ap = _ln_stats(p, ld, xt[b][:], sq_scr, stats,
                               32 + 8 * (t % 2), s_v, s_a, LN_EPS, 1.0 / D)
        p.emit("vector", lambda v, b=b, m_ap=m_ap, r_ap=r_ap:
               v.tensor_scalar(lnx[b][:], xt[b][:], m_ap, r_ap,
                               ALU.subtract, ALU.mult),
               wait=[s_a.now()])
        bank2 = p_tr if t % 2 == 0 else p_trB
        tr_batch(p, ident, bank2,
                 [lnx[b][:, 128 * c:128 * (c + 1)] for c in range(DC)],
                 evacs=tr_views(bank2, bigT[:], DC, t, TOK),
                 evac_engines=("vector", "vector"),
                 first_wait=[(s_v, s_v.n)], extra_wait=[consts_ready])
    lnxT_done = s_v.now()

    def qkv_tile(t, w_sb, bias_off, src_fm, dst, fw=()):
        for half in range(2):
            for c in range(DC):
                p.emit("tensor", lambda te, t=t, c=c, half=half, dst=dst,
                       w_sb=w_sb, src_fm=src_fm: te.matmul(
                           dst[:, 512 * half:512 * (half + 1)],
                           src_fm[:, c * TOK + 128 * t:
                                  c * TOK + 128 * (t + 1)],
                           w_sb[:, c * D + 512 * half:
                                c * D + 512 * (half + 1)],
                           start=(c == 0), stop=False),
                       wait=(list(fw) if (half == 0 and c == 0) else []))
            p.emit("tensor", lambda te, half=half, bias_off=bias_off,
                   dst=dst: te.matmul(
                       dst[:, 512 * half:512 * (half + 1)], onesT[0:1, :],
                       b_rows[:, bias_off + 512 * half:
                              bias_off + 512 * (half + 1)],
                       start=False, stop=True),
                   inc=(s_mm if half == 1 else None))

    last_read = {}
    pend_tr = None
    for t in range(NT):
        dst = sc_e if t % 2 == 0 else sc_o
        fw = [(s_v, s_v.n), (s_a, s_a.n)] if t == 0 else []
        if t >= 2:
            fw = [(s_v, last_read[t - 2])]
        qkv_tile(t, wbA, 0, bigT, dst, fw)
        mmq = s_mm.now()
        if pend_tr is not None:
            pend_tr()
            pend_tr = None
        eld = elu_norm_tile(dst, qn_t, 0, [mmq])
        last_read[t] = s_v.n

        def mk_tr(t=t, eld=eld):
            def go():
                bank = p_tr if t % 2 == 0 else p_trB
                tr_batch(p, ident, bank,
                         [qn_t[:, 128 * c:128 * (c + 1)]
                          for c in range(DC)],
                         evacs=tr_views(bank, qnT[:], DC, t, TOK),
                         evac_engines=("vector", "vector"), first_wait=eld)
            return go
        pend_tr = mk_tr()
    pend_tr()
    q_done2 = tr_done(p)
    q_done = q_done2[0]
    q_mm_done = s_mm.now()

    p.dma("gpsimd", bigT[:].rearrange("p (c f) -> p c f", c=DC),
          srcT.rearrange("(c p) f -> p c f", p=128), s_g,
          wait=[q_mm_done])
    src_ready = s_g.now()
    last_read = {}
    pend_tr = None
    for t in range(NT):
        dst = sc_e if t % 2 == 0 else sc_o
        fw = [src_ready] if t == 0 else []
        if t == 1:
            fw = [(s_v, s_v.n)]
        if t >= 2:
            fw = [(s_v, last_read[t - 2])]
        qkv_tile(t, wbB, D, bigT, dst, fw)
        mmk = s_mm.now()
        if pend_tr is not None:
            pend_tr()
            pend_tr = None
        eld = elu_norm_tile(dst, qn_t, 0, [mmk])
        last_read[t] = s_v.n

        def mk_trk(t=t, eld=eld):
            def go():
                bank = p_tr if t % 2 == 0 else p_trB
                tr_batch(p, ident, bank,
                         [qn_t[:, 128 * c:128 * (c + 1)]
                          for c in range(DC)],
                         evacs=tr_views(bank, knT[:], DC, t, TOK),
                         evac_engines=("vector", "vector"), first_wait=eld)
            return go
        pend_tr = mk_trk()
    pend_tr()
    kn_done2 = tr_done(p)
    kn_done = kn_done2[0]
    kn_mm_done = s_mm.now()
    p.dma("gpsimd", wbA[:].rearrange("p (c f) -> p c f", c=DC),
          wv.rearrange("(c p) f -> p c f", p=128), s_g,
          wait=[q_mm_done])
    wv_ready = s_g.now()

    p.dma("gpsimd", bigT[:].rearrange("p (c f) -> p c f", c=DC),
          valT.rearrange("(c p) f -> p c f", p=128), s_g,
          wait=[kn_mm_done])
    val_ready = s_g.now()
    last_read = {}
    for t in range(NT):
        dst = sc_e if t % 2 == 0 else sc_o
        fw = [val_ready, wv_ready, ones_done, (s_v, s_v.n)] if t <= 1 else \
            [(s_v, last_read[t - 2])]
        qkv_tile(t, wbA, 2 * D, bigT, dst, fw)
        mmv = s_mm.now()
        p.emit("vector", lambda v, t=t, dst=dst: v.tensor_copy(
            vn_f[:, t * NH * 65:(t + 1) * NH * 65].rearrange(
                "p (h s) -> p h s", s=65)[:, :, 0:64],
            dst[:, 0:1024].rearrange("p (h d) -> p h d", d=64)),
            wait=[mmv])
        last_read[t] = s_v.n
    v_done = s_v.now()
    v_mm_done = s_mm.now()
    p.dma("gpsimd", wbB[:].rearrange("p (c f) -> p c f", c=DC),
          wo.rearrange("(c p) f -> p c f", p=128), s_g,
          wait=[kn_mm_done])
    wo_ready = s_g.now()

    ptrB_guard.__exit__(None, None, None)
    ptr_guard.__exit__(None, None, None)
    sco_guard.__exit__(None, None, None)
    sce_guard.__exit__(None, None, None)
    scA = st.enter_context(nc.psum_tensor("CscA", [128, 1024], F32))
    scB = st.enter_context(nc.psum_tensor("CscB", [128, 1024], F32))
    ctx_e = st.enter_context(nc.psum_tensor("Cctx_e", [65, 1024], F32))
    ctx_o = st.enter_context(nc.psum_tensor("Cctx_o", [65, 1024], F32))
    scP = [scA, scB]

    for hp in range(8):
        he, ho = 2 * hp, 2 * hp + 1
        for tqp in range(2):
            exp_done = {}
            av_done = {}
            it5 = 0

            def emit_av5(j):
                sc = j % 16
                tq2 = j // 16
                Ab = Abuf2_pair[j % 2]
                avw = [(s_a, exp_done[j])]
                for par, dstc, h in ((0, ctx_e, he), (64, ctx_o, ho)):
                    p.emit("tensor", lambda te, h=h, sc=sc, tq2=tq2,
                           par=par, dstc=dstc, Ab=Ab: te.matmul(
                               dstc[0:65, 512 * tq2:512 * (tq2 + 1)],
                               vn_f[:, sc * NH * 65 + 65 * h:
                                    sc * NH * 65 + 65 * (h + 1)],
                               Ab[:, 512 * (par // 64):
                                  512 * (par // 64) + 512],
                               start=(sc == 0), stop=(sc == 15)),
                           wait=(avw if par == 0 else []),
                           inc=(s_mm if par == 64 else None))
                av_done[j] = s_mm.n

            for tq2 in range(2):
                tq = 2 * tqp + tq2
                for sc in range(16):
                    dst = scP[it5 % 2]
                    sw = ([(s_v, s_v.n), (s_a, s_a.n), v_done] +
                          q_done2 + kn_done2)
                    if it5 >= 2:
                        sw = [(s_a, exp_done[it5 - 2])]
                    for par, h in ((0, he), (64, ho)):
                        p.emit("tensor", lambda te, h=h, sc=sc, tq=tq,
                               par=par, dst=dst: te.matmul(
                                   dst[:, 512 * (par // 64):
                                       512 * (par // 64) + 512],
                                   knT[par:par + 64,
                                       (h // 2) * TOK + 128 * sc:
                                       (h // 2) * TOK + 128 * (sc + 1)],
                                   qnT[par:par + 64,
                                       (h // 2) * TOK + 512 * tq:
                                       (h // 2) * TOK + 512 * (tq + 1)],
                                   start=True, stop=True,
                                   tile_position=(par, 0)),
                               wait=(sw if par == 0 else []),
                               inc=(s_mm if par == 64 else None))
                    scmm = s_mm.now()
                    if it5 >= 1:
                        emit_av5(it5 - 1)
                    Ab = Abuf2_pair[it5 % 2]
                    ew = [scmm]
                    if it5 >= 2:
                        ew.append((s_mm, av_done[it5 - 2]))
                    p.emit("scalar", lambda s, Ab=Ab, dst=dst: s.activation(
                        Ab[:, 0:1024], dst[:, 0:1024], AF.Exp,
                        scale=0.125), wait=ew)
                    exp_done[it5] = s_a.n
                    it5 += 1
            emit_av5(it5 - 1)
            ctx_mm = s_mm.now()
            for par, dstc, h, tag in ((0, ctx_e, he, "e"),
                                      (64, ctx_o, ho, "o")):
                p.emit("vector", lambda v, dstc=dstc: v.tensor_copy(
                    dzf[64:65, 0:1024], dstc[64:65, 0:1024]),
                    wait=[ctx_mm])

                def _recip2(v):
                    with nc.allow_low_precision(
                            reason="bf16 softmax denom within tolerance"):
                        return v.reciprocal(zline[64:65, 0:1024],
                                            dzf[64:65, 0:1024])
                p.emit("vector", _recip2)
                zl = s_v.now()
                for g in range(2):
                    p.emit("tensor", lambda te, g=g: te.matmul(
                        scA[0:65, 512 * g:512 * (g + 1)],
                        onesT[64:65, 0:65],
                        zline[64:65, 512 * g:512 * (g + 1)],
                        start=True, stop=True, tile_position=(64, 0)),
                        wait=([zl, (s_a, s_a.n)] if g == 0 else []),
                        inc=(s_mm if g == 1 else None))
                p.emit("vector", lambda v: v.tensor_copy(
                    z_sb[0:65, 0:1024], scA[0:65, 0:1024]),
                    wait=[s_mm.now()])
                if tag == "e":
                    p.emit("vector", lambda v, hp=hp, tqp=tqp, dstc=dstc:
                           v.tensor_tensor(
                               bigT[0:64, hp * TOK + 1024 * tqp:
                                    hp * TOK + 1024 * (tqp + 1)],
                               dstc[0:64, 0:1024], z_sb[0:64, 0:1024],
                               ALU.mult))
                else:
                    p.emit("vector", lambda v, dstc=dstc:
                           v.tensor_tensor(
                               tmp_o[0:64, 0:1024], dstc[0:64, 0:1024],
                               z_sb[0:64, 0:1024], ALU.mult),
                           wait=[(s_ctx, s_ctx.n)])
                    p.dma("sync", bigT[64:128, hp * TOK + 1024 * tqp:
                                       hp * TOK + 1024 * (tqp + 1)],
                          tmp_o[0:64, 0:1024], s_ctx,
                          wait=[(s_v, s_v.n)])
    ctx_done_v = s_v.now()
    ctx_done_dma = s_ctx.now()
    u_g.__exit__(None, None, None)
    ab_g.__exit__(None, None, None)
    vnf_g.__exit__(None, None, None)
    knT_g.__exit__(None, None, None)
    qnT_g.__exit__(None, None, None)
    p6_guards = []

    def _g6(name, shape, dt):
        g = nc.sbuf_tensor(name, shape, dt)
        p6_guards.append(g)
        return g.__enter__()
    xt6 = [_g6(f"Cxt6_{i}", [128, D], F32) for i in range(2)]
    ysb6 = [_g6(f"Cysb6_{i}", [128, D], F32) for i in range(2)]
    tmp6 = [_g6(f"Ctmp6_{i}", [128, D], F32) for i in range(2)]
    lnx6 = [_g6(f"Clnx6_{i}", [128, D], BF16) for i in range(2)]
    sq6 = _g6("Csq6", [128, D], BF16)

    store_done = {}
    resid_done = {}
    xatt_done = {}
    pend6 = None
    for t in range(NT):
        b = t % 2
        xw0 = [(s_a, s_a.n)]
        if t >= 2:
            xw0.append((s_v, resid_done[t - 2]))
        elif t == 0:
            xw0.append((s_v, s_v.n))
        p.dma("sync", xt6[b][:], x2i[128 * t:128 * (t + 1), :], s_ld[b],
              wait=xw0)
        xld = [s_ld[b].now()]
        pso = sc_e if t % 2 == 0 else sc_o
        for half in range(2):
            if half == 0:
                how = [wo_ready, gb_done]
                if t == 0:
                    how += [ctx_done_v, ctx_done_dma, (s_v, s_v.n),
                            (s_a, s_a.n)]
                elif t >= 2:
                    how.append((s_v, xatt_done[t - 2]))
            else:
                how = []
            for c in range(DC):
                p.emit("tensor", lambda te, t=t, c=c, half=half, pso=pso:
                       te.matmul(
                           pso[:, 512 * half:512 * (half + 1)],
                           bigT[:, c * TOK + 128 * t:
                                c * TOK + 128 * (t + 1)],
                           wbB[:, c * D + 512 * half:
                               c * D + 512 * (half + 1)],
                           start=(c == 0), stop=False),
                       wait=(how if c == 0 else []))
            p.emit("tensor", lambda te, half=half, pso=pso: te.matmul(
                pso[:, 512 * half:512 * (half + 1)], onesT[0:1, :],
                b_rows[:, 3 * D + 512 * half:3 * D + 512 * (half + 1)],
                start=False, stop=True), inc=s_mm)
        wo_mm = s_mm.now()
        if pend6 is not None:
            pend6()
            pend6 = None

        def mk_tail6(t=t, b=b, pso=pso, xld=xld, wo_mm=wo_mm):
            def go():
                m_ap, r_ap = _ln_stats(p, xld, xt6[b][:], sq6[:], stats,
                                       32 + 8 * (t % 2), s_v, s_a, LN_EPS,
                                       1.0 / D)
                p.emit("vector", lambda v, m_ap=m_ap, r_ap=r_ap:
                       v.tensor_scalar(lnx6[b][:], xt6[b][:], m_ap, r_ap,
                                       ALU.subtract, ALU.mult),
                       wait=[s_a.now()])
                for half in range(2):
                    xw = [wo_mm]
                    if t >= 2 and half == 0:
                        xw.append((s_st[t % 2], store_done[t - 2]))
                    p.emit("vector", lambda v, half=half: v.tensor_tensor(
                        ysb6[b][:, 512 * half:512 * (half + 1)],
                        pso[:, 512 * half:512 * (half + 1)],
                        lnx6[b][:, 512 * half:512 * (half + 1)], ALU.add),
                        wait=xw)
                xatt_done[t] = s_v.n
                x_att = s_v.now()
                m_ap, r_ap = _ln_stats(p, [x_att], ysb6[b][:], sq6[:], stats,
                                       32 + 8 * (t % 2), s_v, s_a,
                                       L1_LN_EPS, 1.0 / D)
                p.emit("vector", lambda v, m_ap=m_ap, r_ap=r_ap:
                       v.tensor_scalar(tmp6[b][:], ysb6[b][:], m_ap, r_ap,
                                       ALU.subtract, ALU.mult),
                       wait=[s_a.now()])
                p.emit("vector", lambda v: v.tensor_tensor(
                    tmp6[b][:], tmp6[b][:], grepl[:], ALU.mult))
                p.emit("vector", lambda v: v.tensor_tensor(
                    tmp6[b][:], tmp6[b][:], brepl[:], ALU.add))
                hln = s_v.now()
                m_ap, r_ap = _ln_stats(p, [hln], tmp6[b][:], sq6[:], stats,
                                       32 + 8 * (t % 2), s_v, s_a, LN_EPS,
                                       1.0 / D)
                p.emit("vector", lambda v, m_ap=m_ap, r_ap=r_ap:
                       v.tensor_scalar(ysb6[b][:], tmp6[b][:], m_ap, r_ap,
                                       ALU.subtract, ALU.mult),
                       wait=[s_a.now()])
                p.emit("vector", lambda v: v.tensor_tensor(
                    ysb6[b][:], ysb6[b][:], xt6[b][:], ALU.add))
                resid_done[t] = s_v.n
                p.dma("sync", x3o[128 * t:128 * (t + 1), :], ysb6[b][:],
                      s_st[b], wait=[s_v.now()])
                store_done[t] = s_st[b].n
            return go
        pend6 = mk_tail6()
    pend6()
    p.emit("sync", lambda sy: sy.nop(),
           wait=[s_st[0].now(), s_st[1].now()])
    for g in reversed(p6_guards):
        g.__exit__(None, None, None)
    bigT_g.__exit__(None, None, None)


def build_C():
    nc = bass.Bass("TRN2", target_bir_lowering=False, debug=False)
    io = declare_io_C(nc)
    st = contextlib.ExitStack()
    with st:
        p = Prog(nc, st)
        make_sems(p)
        emit_C(nc, st, p, io)
        p.finalize()
    return nc


def _prep_C(inputs, core, x2=None):
    t0 = core * TOK
    d = {
        "srcT": np.ascontiguousarray(
            np.asarray(inputs["L1_attn_source_embedding"]).T),
        "valT": np.ascontiguousarray(
            np.asarray(inputs["L1_attn_value_embedding"]).T),
        "wq": np.ascontiguousarray(np.asarray(inputs["L1_attn_Wq"])),
        "wk": np.ascontiguousarray(np.asarray(inputs["L1_attn_Wk"])),
        "wv": np.ascontiguousarray(np.asarray(inputs["L1_attn_Wv"])),
        "wo": np.ascontiguousarray(np.asarray(inputs["L1_attn_Wo"])),
        "bqr": np.asarray(inputs["L1_attn_bq"]).reshape(1, D),
        "bkr": np.asarray(inputs["L1_attn_bk"]).reshape(1, D),
        "bvr": np.asarray(inputs["L1_attn_bv"]).reshape(1, D),
        "bor": np.asarray(inputs["L1_attn_bo"]).reshape(1, D),
        "lngr": np.asarray(inputs["L1_attn_ln_g"]).reshape(1, D),
        "lnbr": np.asarray(inputs["L1_attn_ln_b"]).reshape(1, D),
    }
    if x2 is not None:
        d["x2i"] = np.ascontiguousarray(x2[t0:t0 + TOK])
    return d


def run_C(inputs, x2):
    nc = build_C()
    in_maps = [{"C" + k: v for k, v in _prep_C(inputs, c, x2).items()} for c in range(NCORES)]
    res = run_bass_kernel_spmd(nc, in_maps, core_ids=list(range(NCORES)))
    return np.concatenate([r["x3"] for r in res.results], axis=0)



def declare_io_D(nc, scratch=None):
    io = {}
    io["x3i"] = (scratch["x3i"] if scratch and "x3i" in scratch else nc.dram_tensor("Dx3i", [TOK, D], F32, kind="ExternalInput").ap())
    io["wgT"] = nc.dram_tensor("DwgT", [D, 3072], I32, kind="ExternalInput").ap()
    io["wuT"] = nc.dram_tensor("DwuT", [D, 3072], I32, kind="ExternalInput").ap()
    io["wdT"] = nc.dram_tensor("DwdT", [3072, D], I32, kind="ExternalInput").ap()
    io["x4o"] = (scratch["x4o"] if scratch and "x4o" in scratch else nc.dram_tensor("Dx4", [TOK, D], F32, kind="ExternalOutput").ap())
    return io


def emit_D(nc, st, p, io):
    x3i = io["x3i"]
    wgT = io["wgT"]
    wuT = io["wuT"]
    wdT = io["wdT"]
    x4o = io["x4o"]
    sb = lambda name, shape, dt=BF16: st.enter_context(
        nc.sbuf_tensor("DD" + name, shape, dt))
    wg_s = sb("wg_s", [128, DC * 3072])
    wu_s = sb("wu_s", [128, DC * 3072])
    wd_s = sb("wd_s", [128, 24 * D])
    ident = sb("ident", [128, 128])
    lnxT_t = [sb("lnxT_t0", [128, DC * 128]), sb("lnxT_t1", [128, DC * 128])]
    gate_s2 = [sb("gate_s0", [128, 1024]), sb("gate_s1", [128, 1024])]
    h_bf2 = [sb("h_bf0", [128, 1024]), sb("h_bf1", [128, 1024])]
    hT_sb2 = [sb("hT_sb0", [128, 1024]), sb("hT_sb1", [128, 1024])]
    xt = [sb(f"xt{i}", [128, D], F32) for i in range(2)]
    lnx = [sb(f"lnx{i}", [128, D]) for i in range(2)]
    ysb = [sb(f"ysb{i}", [128, D], F32) for i in range(2)]
    sq_scr = sb("sq_scr", [128, 1536])
    stats = sb("stats", [128, 16], F32)
    ps = lambda name, shape, dt=F32: st.enter_context(
        nc.psum_tensor("DD" + name, shape, dt))
    psum_gA = ps("psum_gA", [128, 1024])
    psum_gB = ps("psum_gB", [128, 1024])
    psum_y = ps("psum_y", [128, 1024])
    p_trA = ps("p_trA", [128, 1024], BF16)
    p_trB = ps("p_trB", [128, 1024], BF16)

    S = p.S
    s_ld = S.s_ld
    s_w = S.s_w
    s_g = S.s_g
    s_mm = S.s_mm
    s_v = S.s_v
    s_a = S.s_a
    s_st = S.s_st

    p.dma("sync", ident[:], nc.inline_tensor(IDENT_BF, "identD").ap(), s_w)
    consts_ready = s_w.now()
    p.dma("gpsimd", wg_s[:].rearrange("p (c f) -> p c f", c=DC),
          wgT.rearrange("(c p) f -> p c f", p=128), s_g)
    p.dma("gpsimd", wu_s[:].rearrange("p (c f) -> p c f", c=DC),
          wuT.rearrange("(c p) f -> p c f", p=128), s_g)
    for blk in range(3):
        p.dma("gpsimd",
              wd_s[:, blk * 8 * D:(blk + 1) * 8 * D].rearrange(
                  "p (c f) -> p c f", c=8),
              wdT[blk * 1024:(blk + 1) * 1024, :].rearrange(
                  "(c p) f -> p c f", p=128), s_g)
    w_ready = s_g.now()


    store_done = {}
    y_evac = 0
    norm_done = {}
    upmm_done = {}
    war = {}
    last_silu = [None]
    last_stt = [None]
    last_stt_same = {}
    last_trs_same = {}
    pend_body = None
    for t in range(NT):
        b = t % 2
        xwait = [(s_v, norm_done[t - 2]), (s_a, s_a.n)] if t >= 2 else []
        p.dma("sync", xt[b][:], x3i[128 * t:128 * (t + 1), :], s_ld[b],
              wait=xwait)
        ld = [s_ld[b].now()]
        if pend_body is not None:
            pend_body()
            pend_body = None
        m_ap, r_ap = _ln_stats(p, ld, xt[b][:], sq_scr[:, 0:1024], stats,
                               8 * (t % 2), s_v, s_a, LN_EPS, 1.0 / D,
                               extra_r_scale=1.0 / 127.0)
        stats_a = s_a.now()

        def mk_body(t=t, b=b, m_ap=m_ap, r_ap=r_ap, stats_a=stats_a):
          def go():
            nonlocal y_evac
            p.emit("vector", lambda v: v.tensor_scalar(
                lnx[b][:], xt[b][:], m_ap, r_ap,
                ALU.subtract, ALU.mult), wait=[stats_a])
            norm_done[t] = s_v.n
            trw = [consts_ready]
            if t >= 2:
                trw.append((s_mm, upmm_done[t - 2]))
            tr_rotate(p, ident, [p_trA, p_trB],
                      [(lnx[b][:, 128 * c:128 * (c + 1)],
                        lnxT_t[b][:, 128 * c:128 * (c + 1)]) for c in range(DC)],
                      extra_wait=trw,
                      first_wait=[(s_v, norm_done[t])])
            lnxT_ready2 = tr_done(p)
            y_first = True
            for third in range(3):
                par = third % 2
                gate_s = gate_s2[par]
                h_bf = h_bf2[par]
                hT_sb = hT_sb2[par]
                gw = lnxT_ready2 + [w_ready]
                if last_silu[0] is not None:
                    gw.append((s_a, last_silu[0]))
                for g in range(2):
                    for e in range(DC):
                        p.emit("tensor", lambda te, b=b, third=third, g=g, e=e:
                               te.matmul(
                                   psum_gA[:, 512 * g:512 * (g + 1)],
                                   lnxT_t[b][:, 128 * e:128 * (e + 1)],
                                   wg_s[:, e * 3072 + 1024 * third + 512 * g:
                                        e * 3072 + 1024 * third + 512 * (g + 1)],
                                   start=(e == 0), stop=(e == DC - 1)),
                               wait=(gw if (g == 0 and e == 0) else []),
                               inc=(s_mm if (g == 1 and e == DC - 1) else None))
                gmm = s_mm.now()
                uw = []
                if last_stt[0] is not None:
                    uw.append((s_v, last_stt[0]))
                for g in range(2):
                    for e in range(DC):
                        p.emit("tensor", lambda te, b=b, third=third, g=g, e=e:
                               te.matmul(
                                   psum_gB[:, 512 * g:512 * (g + 1)],
                                   lnxT_t[b][:, 128 * e:128 * (e + 1)],
                                   wu_s[:, e * 3072 + 1024 * third + 512 * g:
                                        e * 3072 + 1024 * third + 512 * (g + 1)],
                                   start=(e == 0), stop=(e == DC - 1)),
                               wait=(uw if (g == 0 and e == 0) else []),
                               inc=(s_mm if (g == 1 and e == DC - 1) else None))
                umm = s_mm.now()
                sw2 = [gmm]
                if (t, third, "stt") in war or True:
                    pass
                if last_stt_same.get(par) is not None:
                    sw2.append((s_v, last_stt_same[par]))
                p.emit("scalar", lambda s, gate_s=gate_s: s.activation(
                    gate_s[:], psum_gA[:], AF.Silu), wait=sw2)
                last_silu[0] = s_a.n
                hww = [umm, (s_a, last_silu[0])]
                if last_trs_same.get(par) is not None:
                    hww.append((s_mm, last_trs_same[par]))
                p.emit("vector", lambda v, gate_s=gate_s, h_bf=h_bf:
                       v.scalar_tensor_tensor(
                           h_bf[:], gate_s[:], 1.0 / 127.0, psum_gB[:],
                           ALU.mult, ALU.mult), wait=hww)
                last_stt[0] = s_v.n
                last_stt_same[par] = s_v.n
                tr_rotate(p, ident, [p_trA, p_trB],
                          [(h_bf[:, 128 * hc:128 * (hc + 1)],
                            hT_sb[:, 128 * hc:128 * (hc + 1)])
                           for hc in range(DC)],
                          first_wait=[(s_v, last_stt[0])])
                last_trs_same[par] = s_mm.n
                hT_ready2 = tr_done(p)
                for hc in range(DC):
                    for g in range(2):
                        dw = []
                        if hc == 0 and g == 0:
                            dw = list(hT_ready2)
                            if y_first and t > 0:
                                dw.append((s_v, y_evac))
                        p.emit("tensor", lambda te, third=third, hc=hc, g=g,
                               y_first=y_first, hT_sb=hT_sb: te.matmul(
                                   psum_y[:, 512 * g:512 * (g + 1)],
                                   hT_sb[:, 128 * hc:128 * (hc + 1)],
                                   wd_s[:, (8 * third + hc) * D + 512 * g:
                                        (8 * third + hc) * D + 512 * (g + 1)],
                                   start=(y_first and hc == 0),
                                   stop=(third == 2 and hc == DC - 1)),
                               wait=dw,
                               inc=(s_mm if (third == 2 and hc == DC - 1 and
                                             g == 1) else None))
                y_first = False
            upmm_done[t] = s_mm.n
            ymm = s_mm.now()
            hw0 = [ymm]
            if t >= 2:
                hw0.append((s_st[t % 2], store_done[t - 2]))
            p.emit("vector", lambda v, b=b: v.tensor_copy(
                ysb[b][:], psum_y[:]), wait=hw0)
            y_evac = s_v.n
            m_ap, r_ap = _ln_stats(p, [s_v.now()], ysb[b][:],
                                   sq_scr[:, 0:1024], stats, 8 * (t % 2),
                                   s_v, s_a, LN_EPS, 1.0 / D)
            p.emit("vector", lambda v, b=b, m_ap=m_ap, r_ap=r_ap:
                   v.tensor_scalar(ysb[b][:], ysb[b][:], m_ap, r_ap,
                                   ALU.subtract, ALU.mult), wait=[s_a.now()])
            p.emit("vector", lambda v, b=b: v.tensor_tensor(
                ysb[b][:], ysb[b][:], xt[b][:], ALU.add))
            p.dma("sync", x4o[128 * t:128 * (t + 1), :], ysb[b][:], s_st[b],
                  wait=[s_v.now()])
            store_done[t] = s_st[b].n
          return go
        pend_body = mk_body()
    pend_body()
    p.emit("sync", lambda sy: sy.nop(),
           wait=[s_st[0].now(), s_st[1].now()])


def build_D():
    nc = bass.Bass("TRN2", target_bir_lowering=False, debug=False)
    io = declare_io_D(nc)
    st = contextlib.ExitStack()
    with st:
        p = Prog(nc, st)
        make_sems(p)
        emit_D(nc, st, p, io)
        p.finalize()
    return nc


def _prep_D(inputs, core, x3=None):
    t0 = core * TOK
    d = {
        "wgT": np.ascontiguousarray(np.asarray(inputs["L1_ffn_w_gate"]).T),
        "wuT": np.ascontiguousarray(np.asarray(inputs["L1_ffn_w_up"]).T),
        "wdT": np.ascontiguousarray(np.asarray(inputs["L1_ffn_w_down"]).T),
    }
    if x3 is not None:
        d["x3i"] = np.ascontiguousarray(x3[t0:t0 + TOK])
    return d


def run_D(inputs, x3):
    nc = build_D()
    in_maps = [{"D" + k: v for k, v in _prep_D(inputs, c, x3).items()} for c in range(NCORES)]
    res = run_bass_kernel_spmd(nc, in_maps, core_ids=list(range(NCORES)))
    return np.concatenate([r["x4"] for r in res.results], axis=0)


def build_full():
    nc = bass.Bass("TRN2", target_bir_lowering=False, debug=False)
    x1s = nc.dram_tensor("x1s", [TOK, D], F32, kind="Internal").ap()
    x2s = nc.dram_tensor("x2s", [TOK, D], F32, kind="Internal").ap()
    x3s = nc.dram_tensor("x3s", [TOK, D], F32, kind="Internal").ap()
    ios = [
        declare_io_A(nc, scratch={"x1o": x1s}),
        declare_io_B(nc, scratch={"x1i": x1s, "x2o": x2s}),
        declare_io_C(nc, scratch={"x2i": x2s, "x3o": x3s}),
        declare_io_D(nc, scratch={"x3i": x3s}),
    ]
    emits = [emit_A, emit_B, emit_C, emit_D]
    with contextlib.ExitStack() as st0:
        p = Prog(nc, st0)
        make_sems(p)
        for i, (em, io) in enumerate(zip(emits, ios)):
            with contextlib.ExitStack() as stX:
                em(nc, stX, p, io)
            if i < 3:
                barrier(p)
        p.finalize()
    return nc


def kernel(**inputs):
    nc = build_full()
    in_maps = []
    for c in range(NCORES):
        m = {}
        for X, d in (("A", _prep_A(inputs, c)), ("B", _prep_B(inputs, c)),
                     ("C", _prep_C(inputs, c)), ("D", _prep_D(inputs, c))):
            for k, v in d.items():
                m[X + k] = v
        in_maps.append(m)
    res = run_bass_kernel_spmd(nc, in_maps, core_ids=list(range(NCORES)))
    x4 = np.concatenate([r["Dx4"] for r in res.results], axis=0)
    return x4.reshape(4, LBATCH, D)


def kernel_multilaunch(**inputs):
    x1 = run_A(inputs)
    x2 = run_B(inputs, x1)
    x3 = run_C(inputs, x2)
    x4 = run_D(inputs, x3)
    return x4.reshape(4, LBATCH, D)
